# version 01
# baseline (speedup 1.0000x reference)
import math
from contextlib import ExitStack


class _StopBuild(Exception):
    pass

import numpy as np
import ml_dtypes

import concourse.bass as bass
import concourse.mybir as mybir
from concourse.bass_utils import run_bass_kernel_spmd

F32 = mybir.dt.float32
BF16 = mybir.dt.bfloat16
U8 = mybir.dt.uint8
AF = mybir.ActivationFunctionType
ALU = mybir.AluOpType

B, S, D = 2, 8192, 1024
TPC = 2048
H, HD, DL = 16, 64, 256
LAM = 0.01
DF = 4096
TT = 512
NTT = TPC // TT
EPS = 1e-5
QT = 512
NQT = S // QT
NKC = S // 128
R3 = 3
DEBUG_SERIAL = [0]
DEBUG_SKIP = set()
FLUSH_INCS = [True]

ENGS = ("pe", "act", "dve", "pool", "sp")
DQS = ("dsp", "dact", "dpool")


class Plan:
    def __init__(self, nc, stack):
        self.nc = nc
        self.stack = stack
        self.ops = {e: [] for e in ENGS}
        self.sem = {}
        for e in ENGS:
            self.sem[e] = stack.enter_context(nc.semaphore("s_" + e))
        self.sem["cc"] = stack.enter_context(nc.semaphore("s_cc"))
        self.cnt = {k: 0 for k in list(ENGS) + ["cc"]}
        self.chans = set()
        self.nops = 0
        self.flush_thunk = {}

    def chan(self, name):
        if name not in self.chans:
            self.sem[name] = self.stack.enter_context(self.nc.semaphore("c_" + name))
            self.cnt[name] = 0
            self.chans.add(name)
        return name

    def set_flush(self, fl_sb):
        self.flush_thunk = {
            "dve": (lambda e: e.tensor_copy(fl_sb[0:1, 0:1], fl_sb[0:1, 1:2])),
            "act": (lambda e: e.copy(fl_sb[0:1, 2:3], fl_sb[0:1, 3:4])),
        }

    def op(self, eng, thunk, waits=(), inc=False):
        ws = tuple((k, v) for (k, v) in waits if v > 0 and k != eng)
        if inc and FLUSH_INCS[0] and eng in self.flush_thunk:
            self.ops[eng].append((ws, thunk, None))
            self.cnt[eng] += 1
            self.ops[eng].append(((), self.flush_thunk[eng], (eng, 1)))
            self.nops += 2
            return (eng, self.cnt[eng])
        incspec = None
        if inc:
            self.cnt[eng] += 1
            incspec = (eng, 1)
        self.ops[eng].append((ws, thunk, incspec))
        self.nops += 1
        return (eng, self.cnt[eng])

    def dma(self, q, out, in_, waits=(), chan="misc", **kw):
        eng = {"dsp": "sp", "dact": "act", "dpool": "pool"}[q]
        c = self.chan(chan)
        ws = tuple((k, v) for (k, v) in list(waits) + [(c, self.cnt[c])] if v > 0)
        self.cnt[c] += 16

        def t(e, out=out, in_=in_, kw=kw):
            return e.dma_start(out=out, in_=in_, **kw)

        self.ops[eng].append((ws, t, (c, 16)))
        self.nops += 1
        return (c, self.cnt[c])

    def collective(self, kind, op_, groups, ins, outs, waits=()):
        ws = tuple((k, v) for (k, v) in waits if v > 0)
        self.cnt["cc"] += 1

        def t(e, kind=kind, op_=op_, groups=groups, ins=ins, outs=outs):
            return e.collective_compute(kind, op_, replica_groups=groups, ins=ins, outs=outs)

        self.ops["pool"].append((ws, t, ("cc", 1)))
        self.nops += 1
        return ("cc", self.cnt["cc"])

    def barrier(self):
        for e in ENGS:
            for j in range(len(self.ops[e]) - 1, -1, -1):
                ws_j, th_j, inc_j = self.ops[e][j]
                if th_j is None:
                    continue
                if inc_j is not None and inc_j[0] in self.chans:
                    continue
                if inc_j is None:
                    self.cnt[e] += 1
                    self.ops[e][j] = (ws_j, th_j, (e, 1))
                break
        snap = dict(self.cnt)
        for e in ENGS:
            ws = [(k, snap[k]) for k in ENGS if k != e]
            ws += [(c, snap[c]) for c in self.chans]
            ws += [("cc", snap["cc"])]
            ws = tuple((k, v) for (k, v) in ws if v > 0)
            self.ops[e].append((ws, None, None))
            self.nops += 1

    def emit(self, block):
        def runner(ops):
            def f(eng):
                guaranteed = {}
                for ws, thunk, incspec in ops:
                    for (k, v) in ws:
                        if guaranteed.get(k, -1) >= v:
                            continue
                        eng.wait_ge(self.sem[k], v)
                        guaranteed[k] = v
                    if thunk is None:
                        continue
                    inst = thunk(eng)
                    if incspec is not None:
                        inst.then_inc(self.sem[incspec[0]], incspec[1])
            return f
        block.tensor(runner(self.ops["pe"]))
        block.scalar(runner(self.ops["act"]))
        block.vector(runner(self.ops["dve"]))
        block.gpsimd(runner(self.ops["pool"]))
        block.sync(runner(self.ops["sp"]))


def tmax(*toks):
    d = {}
    for t in toks:
        if t is None:
            continue
        items = t if isinstance(t, list) else [t]
        for it_ in items:
            if it_ is None:
                continue
            k, v = it_
            d[k] = max(d.get(k, 0), v)
    return list(d.items())


def build_nc(dump=None):
    nc = bass.Bass("TRN2", target_bir_lowering=False)

    xin = nc.dram_tensor("xin", [TPC, D], F32, kind="ExternalInput").ap()
    wq = nc.dram_tensor("wq", [D, 256], F32, kind="ExternalInput").ap()
    wk = nc.dram_tensor("wk", [D, D], F32, kind="ExternalInput").ap()
    wv = nc.dram_tensor("wv", [D, D], F32, kind="ExternalInput").ap()
    wo = nc.dram_tensor("wo", [256, D], F32, kind="ExternalInput").ap()
    wd = nc.dram_tensor("wd", [D, DL], F32, kind="ExternalInput").ap()
    wuk = nc.dram_tensor("wuk", [DL, 256], F32, kind="ExternalInput").ap()
    wuv = nc.dram_tensor("wuv", [DL, 256], F32, kind="ExternalInput").ap()
    k0u = nc.dram_tensor("k0u", [2 * D, DF], U8, kind="ExternalInput").ap()
    k1u = nc.dram_tensor("k1u", [DF, DF], U8, kind="ExternalInput").ap()
    k2u = nc.dram_tensor("k2u", [DF, D], U8, kind="ExternalInput").ap()
    s0in = nc.dram_tensor("s0in", [128, 1], F32, kind="ExternalInput").ap()
    s1in = nc.dram_tensor("s1in", [128, 1], F32, kind="ExternalInput").ap()
    s2in = nc.dram_tensor("s2in", [128, 1], F32, kind="ExternalInput").ap()
    ident_in = nc.dram_tensor("ident", [128, 128], F32, kind="ExternalInput").ap()
    qsc_in = nc.dram_tensor("qsc_i", [128, 2 * QT], F32, kind="ExternalInput").ap()
    latsc_in = nc.dram_tensor("latsc_i", [128, 128], F32, kind="ExternalInput").ap()
    vundo_in = nc.dram_tensor("vundo_i", [128, 1], F32, kind="ExternalInput").ap()
    corr_in = nc.dram_tensor("corr_i", [128, 4 * QT], F32, kind="ExternalInput").ap()
    svec_in = nc.dram_tensor("svec_i", [128, 64], F32, kind="ExternalInput").ap()
    ldsc_in = nc.dram_tensor("ldsc_i", [128, 64], F32, kind="ExternalInput").ap()
    y = nc.dram_tensor("y", [TPC, D], F32, kind="ExternalOutput").ap()

    x_dram = nc.dram_tensor("x_dram", [D, TPC], F32)
    k0f_dram = nc.dram_tensor("k0f_dram", [D, DF], BF16)
    w1_dram = nc.dram_tensor("w1_dram", [DF, DF], BF16)
    w2_dram = nc.dram_tensor("w2_dram", [DF, D], BF16)
    ag1in = nc.dram_tensor("ag1in", [1280, TPC], BF16)
    ag1out = nc.dram_tensor("ag1out", [4 * 1280, TPC], BF16)
    atmy_dram = nc.dram_tensor("atmy_dram", [256, S], BF16)
    rsin = nc.dram_tensor("rsin", [S, D], BF16)
    rsout = nc.dram_tensor("rsout", [TPC, D], BF16)
    groups = [[0, 1, 2, 3], [4, 5, 6, 7]]
    dbg = None
    if dump is not None:
        dbg = nc.dram_tensor("dbg", list(dump[1]),
                             dump[2] if len(dump) > 2 else F32, kind="ExternalOutput").ap()

    xfm = x_dram.ap().rearrange("(i p) t -> p i t", p=128)
    k0fv = k0f_dram.ap().rearrange("(i p) c -> p i c", p=128)
    w1v = w1_dram.ap().rearrange("(i p) c -> p i c", p=128)
    w2v = w2_dram.ap().rearrange("(i p) c -> p i c", p=128)
    ag1inx = ag1in.ap()[0:D, :].rearrange("(i p) t -> p i t", p=128)
    ag1inl = ag1in.ap()[D:1280, :].rearrange("(i p) t -> p i t", p=128)
    ag1v = ag1out.ap().rearrange("(r i p) t -> r p i t", r=4, p=128)
    atmyv = atmy_dram.ap().rearrange("(i p) t -> p i t", p=128)

    with ExitStack() as st0:
        E = st0.enter_context
        pl = Plan(nc, st0)
        block = E(nc.Block())
        fl_sb = E(nc.sbuf_tensor("fl_sb", [1, 4], F32))
        pl.set_flush(fl_sb)
        pl.op("dve", lambda e: e.memset(fl_sb[0:1, :], 0.0))

        id_f = E(nc.sbuf_tensor("id_f", [128, 128], F32))
        ones_bf = E(nc.sbuf_tensor("ones_bf", [128, 128], BF16))
        m64 = E(nc.sbuf_tensor("m64", [128, 64], F32))
        qsc = E(nc.sbuf_tensor("qsc", [128, 2 * QT], F32))
        latsc = E(nc.sbuf_tensor("latsc", [128, 128], F32))
        vundo = E(nc.sbuf_tensor("vundo", [128, 1], F32))
        corr = E(nc.sbuf_tensor("corr", [128, 4 * QT], F32))
        svec = E(nc.sbuf_tensor("svec", [128, 64], F32))
        ldsc = E(nc.sbuf_tensor("ldsc", [128, 64], F32))
        lnmisc = E(nc.sbuf_tensor("lnmisc", [128, 3], F32))
        s0x4 = E(nc.sbuf_tensor("s0x4", [128, 1], F32))
        s1x2 = E(nc.sbuf_tensor("s1x2", [128, 1], F32))
        s2h = E(nc.sbuf_tensor("s2h", [128, 1], F32))
        psS = [E(nc.psum_tensor(f"psS{i}", [128, 1024], F32)) for i in range(R3)]
        psO = E(nc.psum_tensor("psO", [128, 1024], F32))
        psv = [p[:, :].rearrange("p (o t) -> p o t", o=2) for p in psS]

        t_con = [pl.dma("dsp", id_f[:, :], ident_in, chan="cst"),
                 pl.dma("dsp", qsc[:, :], qsc_in, chan="cst"),
                 pl.dma("dsp", latsc[:, :], latsc_in, chan="cst"),
                 pl.dma("dsp", vundo[:, :], vundo_in, chan="cst"),
                 pl.dma("dsp", corr[:, :], corr_in, chan="cst"),
                 pl.dma("dsp", svec[:, :], svec_in, chan="cst"),
                 pl.dma("dsp", ldsc[:, :], ldsc_in, chan="cst")]
        t_s0 = pl.dma("dsp", s0x4[:, :], s0in, chan="cst")
        t_s1 = pl.dma("dsp", s1x2[:, :], s1in, chan="cst")
        t_s2 = pl.dma("dsp", s2h[:, :], s2in, chan="cst")
        pl.op("dve", lambda e: e.memset(ones_bf[:, :], 1.0), inc=True)
        pl.op("dve", lambda e: e.memset(lnmisc[:, 0:1], EPS), inc=True)
        pl.op("dve", lambda e: e.memset(lnmisc[:, 1:2], -0.5), inc=True)
        pl.op("dve", lambda e: e.memset(lnmisc[:, 2:3], 1.0), inc=True)
        pl.op("dve", lambda e: e.memset(m64[:, :], 1.0 / 64.0), inc=True)
        pl.op("dve", lambda e: e.tensor_scalar(s0x4[:, :], s0x4[:, :], 4.0, None, ALU.mult),
              waits=tmax(t_s0))
        pl.op("dve", lambda e: e.tensor_scalar(s1x2[:, :], s1x2[:, :], 2.0, None, ALU.mult),
              waits=tmax(t_s1))
        t_sv = pl.op("dve", lambda e: e.tensor_scalar(s2h[:, :], s2h[:, :], 0.5, None, ALU.mult),
                     waits=tmax(t_s2), inc=True)

        if dump is not None and dump[0] == "con":
            pl.barrier()
            pl.dma("dsp", dbg, qsc_in, chan="dbg")
            pl.barrier()
            pl.emit(block)
            return nc, pl.nops

        with ExitStack() as sp0:
            Ep = sp0.enter_context
            tin = [Ep(nc.sbuf_tensor(f"tin{i}", [128, D], F32)) for i in range(2)]
            tout = [Ep(nc.sbuf_tensor(f"tout{i}", [128, 1024], F32)) for i in range(2)]
            tin_tok = [None, None]
            tout_tok = [None, None]
            tens4 = psS + [psO]
            t_evprev = None
            t_idc = tmax(t_con)
            for tch in range(0 if "transpose" in DEBUG_SKIP else TPC // 128):
                bi = tch % 2
                t_ld = pl.dma("dsp", tin[bi][:, :], xin[tch * 128:(tch + 1) * 128, :],
                              waits=tmax(tin_tok[bi]), chan="xl")
                for ic in range(8):
                    tns = tens4[ic // 2]
                    co = (ic % 2) * 512
                    def t_tr(e, ic=ic, bi=bi, tns=tns, co=co):
                        return e.transpose(tns[:, co:co + 128],
                                           tin[bi][:, ic * 128:(ic + 1) * 128], id_f[:, :])
                    w = tmax(t_ld, t_idc, t_evprev) if ic == 0 else ()
                    tk = pl.op("pe", t_tr, waits=w, inc=(ic == 7))
                tin_tok[bi] = tk
                for j in range(4):
                    def t_ev(e, bi=bi, j=j, tns=tens4[j]):
                        src = tns[:, :].rearrange("p (s c) -> p s c", s=2)[:, :, 0:128]
                        dst = tout[bi][:, j * 256:(j + 1) * 256].rearrange(
                            "p (s c) -> p s c", c=128)
                        return e.tensor_copy(dst, src)
                    t_e = pl.op("dve", t_ev,
                                waits=tmax(tk, tout_tok[bi]) if j == 0 else (),
                                inc=(j == 3))
                t_evprev = t_e
                tout_tok[bi] = pl.dma(
                    "dsp", xfm[:, :, tch * 128:(tch + 1) * 128],
                    tout[bi][:, :].rearrange("p (i t) -> p i t", t=128), waits=tmax(t_e),
                    chan="xs")
                if DEBUG_SERIAL[0] >= 3:
                    pl.barrier()

            if dump is not None and dump[0] == "x0t":
                pl.barrier()
                pl.dma("dsp", dbg, x_dram.ap(), chan="dbg")
                pl.barrier()
                pl.emit(block)
                return nc, pl.nops
            cu8 = [Ep(nc.sbuf_tensor(f"cu8_{i}", [128, DF], U8)) for i in range(2)]
            cbf = [Ep(nc.sbuf_tensor(f"cbf_{i}", [128, 2048], BF16)) for i in range(2)]
            cu8_tok = [None, None]
            cbf_tok = [None, None]
            conv_i = [0]
            t_k0_done = [None]
            t_w1_done = [None]
            t_w2_done = [None]

            def conv(src1, src2, dst, ncols, fold):
                bi = conv_i[0] % 2
                conv_i[0] += 1
                t_l = pl.dma("dsp", cu8[bi][:, 0:ncols], src1, waits=tmax(cu8_tok[bi]), chan="cvl")
                wl = [t_l]
                if fold:
                    t_l2 = pl.dma("dsp", cu8[bi][:, ncols:2 * ncols], src2, chan="cvl2")
                    wl.append(t_l2)
                    def tc(e, bi=bi, ncols=ncols):
                        return e.tensor_tensor(cbf[bi][:, 0:ncols], cu8[bi][:, 0:ncols],
                                               cu8[bi][:, ncols:2 * ncols], ALU.subtract)
                else:
                    def tc(e, bi=bi, ncols=ncols):
                        return e.tensor_scalar(cbf[bi][:, 0:ncols], cu8[bi][:, 0:ncols],
                                               2.0, -1.0, ALU.mult, ALU.add)
                t_c = pl.op("dve", tc, waits=tmax(*wl, cbf_tok[bi]), inc=True)
                cu8_tok[bi] = t_c
                t_w = pl.dma("dact", dst, cbf[bi][:, 0:ncols], waits=tmax(t_c), chan="cvs")
                cbf_tok[bi] = t_w
                if DEBUG_SERIAL[0] >= 3:
                    pl.barrier()
                return t_w

            conv_ranges = (0, 0, 0) if "conv" in DEBUG_SKIP else (8, 32, 32)
            for i in range(conv_ranges[0]):
                for cy in range(2):
                    csl = slice(cy * 2048, (cy + 1) * 2048)
                    t_k0_done[0] = conv(k0u[i * 128:(i + 1) * 128, csl],
                                        k0u[D + i * 128:D + (i + 1) * 128, csl],
                                        k0f_dram[i * 128:(i + 1) * 128, csl], 2048, True)
            for i in range(conv_ranges[1]):
                for cy in range(2):
                    csl = slice(cy * 2048, (cy + 1) * 2048)
                    t_w1_done[0] = conv(k1u[i * 128:(i + 1) * 128, csl], None,
                                        w1_dram[i * 128:(i + 1) * 128, csl], 2048, False)
            for i in range(conv_ranges[2]):
                t_w2_done[0] = conv(k2u[i * 128:(i + 1) * 128, :], None,
                                    w2_dram[i * 128:(i + 1) * 128, :], D, False)
            pl.barrier()

        def _dump_and_stop2(src_ap):
            pl.barrier()
            pl.dma("dsp", dbg, src_ap, chan="dbg")
            pl.barrier()
            pl.emit(block)
            return nc, pl.nops

        def _dump_and_stop(src_ap):
            pl.barrier()
            pl.dma("dsp", dbg, src_ap, chan="dbg")
            pl.barrier()
            pl.emit(block)
            return nc, pl.nops

        if dump is not None and dump[0] == "k0f":
            r = _dump_and_stop2(k0f_dram.ap())
            if r:
                return r
        if dump is not None and dump[0] == "w1c":
            r = _dump_and_stop2(w1_dram.ap()[0:1024, :])
            if r:
                return r
        if dump is not None and dump[0] == "w1d":
            r = _dump_and_stop2(w1_dram.ap()[3072:4096, :])
            if r:
                return r
        if dump is not None and dump[0] == "w2c":
            r = _dump_and_stop2(w2_dram.ap())
            if r:
                return r
        if dump is not None and dump[0] == "x0":
            r = _dump_and_stop(x_dram.ap())
            if r:
                return r

        ffn_call = [0]

        def emit_ffn():
            fc = ffn_call[0]
            ffn_call[0] += 1
            with ExitStack() as stf:
                Ef = stf.enter_context
                xt = Ef(nc.sbuf_tensor(f"xt_f{fc}", [128, 8, TT], F32))
                xb = Ef(nc.sbuf_tensor(f"xb_f{fc}", [128, 8, TT], BF16))
                xsq = Ef(nc.sbuf_tensor(f"xsq_f{fc}", [128, 8, TT], BF16))
                mrow = Ef(nc.sbuf_tensor(f"mrow_f{fc}", [128, TT], F32))
                msq = Ef(nc.sbuf_tensor(f"msq_f{fc}", [128, TT], F32))
                varr = Ef(nc.sbuf_tensor(f"varr_f{fc}", [128, TT], F32))
                rstd = Ef(nc.sbuf_tensor(f"rstd_f{fc}", [128, TT], F32))
                h1 = Ef(nc.sbuf_tensor(f"h1_f{fc}", [128, 32, TT], BF16))
                h2 = Ef(nc.sbuf_tensor(f"h2_f{fc}", [128, 32, TT], BF16))
                wr = [Ef(nc.sbuf_tensor(f"wr{i}_f{fc}", [128, 8, 256], BF16)) for i in range(R3)]
                wr_free = [None] * R3
                ps_free = [None] * R3
                ring = [0]
                t_stats_free = None
                t_prev_store = None
                t_prev_mm0 = None
                m_bc = mrow[:, :].unsqueeze(1).broadcast_to([128, 8, TT])
                r_bc = rstd[:, :].unsqueeze(1).broadcast_to([128, 8, TT])

                for tt_i in range(NTT):
                    tsl = slice(tt_i * TT, (tt_i + 1) * TT)
                    t_x = pl.dma("dsp", xt[:, :, :], xfm[:, :, tsl],
                                 waits=tmax(t_prev_store,
                                            ps_free[0], ps_free[1], ps_free[2]),
                                 chan="xl")
                    t_xb = pl.op("dve", lambda e: e.tensor_copy(xb[:, :, :], xt[:, :, :]),
                                 waits=tmax(t_x, t_prev_mm0), inc=True)
                    t_sq = pl.op("act", lambda e: e.activation(xsq[:, :, :], xb[:, :, :],
                                                               AF.Square),
                                 waits=tmax(t_xb), inc=True)
                    for ic in range(8):
                        def t_s(e, ic=ic):
                            return e.matmul(psO[:, 0:TT], ones_bf[:, :], xb[:, ic, :],
                                            start=(ic == 0), stop=(ic == 7))
                        pl.op("pe", t_s,
                              waits=tmax(t_xb, t_stats_free) if ic == 0 else ())
                    for ic in range(8):
                        def t_s(e, ic=ic):
                            return e.matmul(psO[:, TT:2 * TT], ones_bf[:, :], xsq[:, ic, :],
                                            start=(ic == 0), stop=(ic == 7))
                        tk = pl.op("pe", t_s, waits=tmax(t_sq) if ic == 0 else (),
                                   inc=(ic == 7))
                    t_m = pl.op("dve", lambda e: e.tensor_scalar(
                        mrow[:, :], psO[:, 0:TT], 1.0 / D, None, ALU.mult),
                        waits=tmax(tk), inc=True)
                    pl.op("dve", lambda e: e.tensor_tensor(msq[:, :], mrow[:, :], mrow[:, :],
                                                           ALU.mult))
                    t_v = pl.op("dve", lambda e: e.scalar_tensor_tensor(
                        varr[:, :], psO[:, TT:2 * TT], 1.0 / D, msq[:, :],
                        ALU.mult, ALU.subtract), inc=True)
                    t_stats_free = t_v
                    pl.op("act", lambda e: e.activation(rstd[:, :], varr[:, :], AF.Ln,
                                                        bias=lnmisc[:, 0:1], scale=lnmisc[:, 2:3]), waits=tmax(t_v))
                    t_ln = pl.op("act", lambda e: e.activation(rstd[:, :], rstd[:, :], AF.Exp,
                                                               bias=0.0, scale=lnmisc[:, 1:2]), inc=True)
                    pl.op("dve", lambda e: e.tensor_tensor(xb[:, :, :], xb[:, :, :], m_bc,
                                                           ALU.subtract), waits=tmax(t_m))
                    t_xn = pl.op("dve", lambda e: e.tensor_tensor(xb[:, :, :], xb[:, :, :],
                                                                  r_bc, ALU.mult),
                                 waits=tmax(t_ln), inc=True)
                    if dump is not None and dump[0] == "ffn_xn" and tt_i == 0:
                        pl.barrier()
                        pl.dma("dsp", dbg.rearrange("(i p) t -> p i t", p=128), xb[:, :, :], chan="dbg")
                        pl.barrier()
                        stop_flag[0] = True
                        return
                    if DEBUG_SERIAL[0]:
                        pl.barrier()

                    for gp in range(16):
                        ri = ring[0] % R3
                        ring[0] += 1
                        t_wl = pl.dma("dsp", wr[ri][:, :, :],
                                      k0fv[:, :, gp * 256:(gp + 1) * 256],
                                      waits=tmax(wr_free[ri], t_k0_done[0]),
                                      chan="wr" + str(ri))
                        ps_i = gp % R3
                        for ic in range(8):
                            for oc in range(2):
                                def t_m0(e, ri=ri, ic=ic, oc=oc, ps_i=ps_i):
                                    return e.matmul(psS[ps_i][:, oc * TT:(oc + 1) * TT],
                                                    wr[ri][:, ic, oc * 128:(oc + 1) * 128],
                                                    xb[:, ic, :],
                                                    start=(ic == 0), stop=(ic == 7))
                                w = tmax(t_wl, t_xn, ps_free[ps_i]) \
                                    if (ic == 0 and oc == 0) else ()
                                tk = pl.op("pe", t_m0, waits=w,
                                           inc=(ic == 7 and oc == 1))
                        wr_free[ri] = tk
                        def t_e0(e, gp=gp, ps_i=ps_i):
                            return e.activation(h1[:, 2 * gp:2 * gp + 2, :], psv[ps_i],
                                                AF.Relu, bias=0.0, scale=s0x4[:, :])
                        ps_free[ps_i] = pl.op("act", t_e0, waits=tmax(tk, t_sv), inc=True)
                        if DEBUG_SERIAL[0] >= 2:
                            pl.barrier()
                    t_prev_mm0 = tk
                    t_h1 = [ps_free[0], ps_free[1], ps_free[2]]
                    if dump is not None and dump[0] == "ffn_h1" and tt_i == 0:
                        pl.barrier()
                        pl.dma("dsp", dbg.rearrange("(i p) t -> p i t", p=128), h1[:, :, :], chan="dbg")
                        pl.barrier()
                        stop_flag[0] = True
                        return
                    if DEBUG_SERIAL[0]:
                        pl.barrier()

                    for gp in range(16):
                        ps_i = gp % R3
                        for icq in range(4):
                            ri = ring[0] % R3
                            ring[0] += 1
                            t_wl = pl.dma("dsp", wr[ri][:, :, :],
                                          w1v[:, icq * 8:(icq + 1) * 8,
                                              gp * 256:(gp + 1) * 256],
                                          waits=tmax(wr_free[ri], t_w1_done[0]),
                                          chan="wr" + str(ri))
                            for ic8 in range(8):
                                ic = icq * 8 + ic8
                                for oc in range(2):
                                    def t_m1(e, ri=ri, ic8=ic8, ic=ic, oc=oc, ps_i=ps_i):
                                        return e.matmul(
                                            psS[ps_i][:, oc * TT:(oc + 1) * TT],
                                            wr[ri][:, ic8, oc * 128:(oc + 1) * 128],
                                            h1[:, ic, :],
                                            start=(ic == 0), stop=(ic == 31))
                                    w = ()
                                    if ic8 == 0 and oc == 0:
                                        w = tmax(t_wl,
                                                 tmax(*t_h1) if gp == 0 and icq == 0 else None,
                                                 ps_free[ps_i] if icq == 0 else None)
                                    tk = pl.op("pe", t_m1, waits=w,
                                               inc=(ic8 == 7 and oc == 1))
                            wr_free[ri] = tk
                        def t_e1(e, gp=gp, ps_i=ps_i):
                            return e.activation(h2[:, 2 * gp:2 * gp + 2, :], psv[ps_i],
                                                AF.Relu, bias=0.0, scale=s1x2[:, :])
                        ps_free[ps_i] = pl.op("act", t_e1, waits=tmax(tk), inc=True)
                        if DEBUG_SERIAL[0] >= 2:
                            pl.barrier()
                    t_h2 = [ps_free[0], ps_free[1], ps_free[2]]
                    if dump is not None and dump[0] == "ffn_h2" and tt_i == 0:
                        pl.barrier()
                        pl.dma("dsp", dbg.rearrange("(i p) t -> p i t", p=128), h2[:, :, :], chan="dbg")
                        pl.barrier()
                        stop_flag[0] = True
                        return
                    if DEBUG_SERIAL[0]:
                        pl.barrier()

                    for gp in range(4):
                        ps_i = gp % R3
                        for icq in range(4):
                            ri = ring[0] % R3
                            ring[0] += 1
                            t_wl = pl.dma("dsp", wr[ri][:, :, :],
                                          w2v[:, icq * 8:(icq + 1) * 8,
                                              gp * 256:(gp + 1) * 256],
                                          waits=tmax(wr_free[ri], t_w2_done[0]),
                                          chan="wr" + str(ri))
                            for ic8 in range(8):
                                ic = icq * 8 + ic8
                                for oc in range(2):
                                    def t_m2(e, ri=ri, ic8=ic8, ic=ic, oc=oc, ps_i=ps_i):
                                        return e.matmul(
                                            psS[ps_i][:, oc * TT:(oc + 1) * TT],
                                            wr[ri][:, ic8, oc * 128:(oc + 1) * 128],
                                            h2[:, ic, :],
                                            start=(ic == 0), stop=(ic == 31))
                                    w = ()
                                    if ic8 == 0 and oc == 0:
                                        w = tmax(t_wl,
                                                 tmax(*t_h2) if gp == 0 and icq == 0 else None,
                                                 ps_free[ps_i] if icq == 0 else None)
                                    tk = pl.op("pe", t_m2, waits=w,
                                               inc=(ic8 == 7 and oc == 1))
                            wr_free[ri] = tk
                        def t_e2(e, gp=gp, ps_i=ps_i):
                            return e.scalar_tensor_tensor(
                                xt[:, 2 * gp:2 * gp + 2, :], psv[ps_i], s2h[:, :],
                                xt[:, 2 * gp:2 * gp + 2, :], ALU.mult, ALU.add)
                        ps_free[ps_i] = pl.op("dve", t_e2, waits=tmax(tk), inc=True)
                        if DEBUG_SERIAL[0] >= 2:
                            pl.barrier()
                    t_prev_store = pl.dma("dsp", xfm[:, :, tsl], xt[:, :, :],
                                          waits=tmax(ps_free[0], ps_free[1], ps_free[2]),
                                          chan="xs")
                    if DEBUG_SERIAL[0]:
                        pl.barrier()
                pl.barrier()

        stop_flag = [False]
        emit_ffn()
        if stop_flag[0]:
            pl.emit(block)
            return nc, pl.nops
        if dump is not None and dump[0] == "x1":
            r = _dump_and_stop(x_dram.ap())
            if r:
                return r

        def emit_wload(pl_, wstg, dram_ap, dest_sb, n_ic, C, dest_off=0):
            stg_tok = [None, None]
            tk = None
            for ic in range(n_ic):
                bi = ic % 2
                t_l = pl_.dma("dsp", wstg[bi][:, 0:C],
                              dram_ap[ic * 128:(ic + 1) * 128, :],
                              waits=tmax(stg_tok[bi]), chan="cst")
                def t_cv(e, bi=bi, ic=ic, C=C, wstg=wstg, dest_sb=dest_sb,
                         dest_off=dest_off):
                    return e.tensor_copy(dest_sb[:, ic, dest_off:dest_off + C],
                                         wstg[bi][:, 0:C])
                tk = pl_.op("dve", t_cv, waits=tmax(t_l), inc=True)
                stg_tok[bi] = tk
            return tk

        with ExitStack() as st2:
            E2 = st2.enter_context
            xn2 = E2(nc.sbuf_tensor("xn2", [128, 8, TPC], BF16))
            lat = E2(nc.sbuf_tensor("lat", [128, 2, TT], BF16))
            xt = E2(nc.sbuf_tensor("xt2", [128, 8, TT], F32))
            xsq = E2(nc.sbuf_tensor("xsq2", [128, 8, TT], BF16))
            mrow = E2(nc.sbuf_tensor("mrow2", [128, TT], F32))
            msq = E2(nc.sbuf_tensor("msq2", [128, TT], F32))
            varr = E2(nc.sbuf_tensor("varr2", [128, TT], F32))
            rstd = E2(nc.sbuf_tensor("rstd2", [128, TT], F32))
            kvt = E2(nc.sbuf_tensor("kvt", [128, 8, TT], BF16))
            wk_sb = E2(nc.sbuf_tensor("wk_sb", [128, 8, D], BF16))
            wv_sb = E2(nc.sbuf_tensor("wv_sb", [128, 8, D], BF16))
            wd_sb = E2(nc.sbuf_tensor("wd_sb", [128, 8, DL], BF16))
            m_bc = mrow[:, :].unsqueeze(1).broadcast_to([128, 8, TT])
            r_bc = rstd[:, :].unsqueeze(1).broadcast_to([128, 8, TT])

            wstg2 = [E2(nc.sbuf_tensor(f"wstg2_{i}", [128, D], F32)) for i in range(2)]
            t_wk = emit_wload(pl, wstg2, wk, wk_sb, 8, D)
            t_wv = emit_wload(pl, wstg2, wv, wv_sb, 8, D)
            t_wd = emit_wload(pl, wstg2, wd, wd_sb, 8, DL)
            ps_free = [None] * R3
            t_stats_free = None
            t_kvt_free = None
            t_lat_free = None
            for tt_i in range(NTT):
                tsl = slice(tt_i * TT, (tt_i + 1) * TT)
                xbv = xn2[:, :, tsl]
                t_x = pl.dma("dsp", xt[:, :, :], xfm[:, :, tsl], chan="xl")
                t_xb = pl.op("dve", lambda e, xbv=xbv: e.tensor_copy(xbv, xt[:, :, :]),
                             waits=tmax(t_x), inc=True)
                t_sq = pl.op("act", lambda e, xbv=xbv: e.activation(xsq[:, :, :], xbv,
                                                                    AF.Square),
                             waits=tmax(t_xb), inc=True)
                for ic in range(8):
                    def t_s(e, ic=ic, xbv=xbv):
                        return e.matmul(psO[:, 0:TT], ones_bf[:, :], xbv[:, ic, :],
                                        start=(ic == 0), stop=(ic == 7))
                    pl.op("pe", t_s, waits=tmax(t_xb, t_stats_free) if ic == 0 else ())
                for ic in range(8):
                    def t_s(e, ic=ic):
                        return e.matmul(psO[:, TT:2 * TT], ones_bf[:, :], xsq[:, ic, :],
                                        start=(ic == 0), stop=(ic == 7))
                    tk = pl.op("pe", t_s, waits=tmax(t_sq) if ic == 0 else (), inc=(ic == 7))
                t_m = pl.op("dve", lambda e: e.tensor_scalar(mrow[:, :], psO[:, 0:TT],
                                                             1.0 / D, None, ALU.mult),
                            waits=tmax(tk), inc=True)
                pl.op("dve", lambda e: e.tensor_tensor(msq[:, :], mrow[:, :], mrow[:, :],
                                                       ALU.mult))
                t_v = pl.op("dve", lambda e: e.scalar_tensor_tensor(
                    varr[:, :], psO[:, TT:2 * TT], 1.0 / D, msq[:, :],
                    ALU.mult, ALU.subtract), inc=True)
                t_stats_free = t_v
                pl.op("act", lambda e: e.activation(rstd[:, :], varr[:, :], AF.Ln,
                                                    bias=lnmisc[:, 0:1], scale=lnmisc[:, 2:3]), waits=tmax(t_v))
                t_ln = pl.op("act", lambda e: e.activation(rstd[:, :], rstd[:, :], AF.Exp,
                                                           bias=0.0, scale=lnmisc[:, 1:2]), inc=True)
                pl.op("dve", lambda e, xbv=xbv: e.tensor_tensor(xbv, xbv, m_bc, ALU.subtract),
                      waits=tmax(t_m))
                t_xn = pl.op("dve", lambda e, xbv=xbv: e.tensor_tensor(xbv, xbv, r_bc,
                                                                       ALU.mult),
                             waits=tmax(t_ln), inc=True)
                pl.dma("dsp", ag1inx[:, :, tsl], xbv, waits=tmax(t_xn), chan="ag")
                for gp in range(4):
                    ps_i = gp % R3
                    for wsel in range(2):
                        wsb = wk_sb if wsel == 0 else wv_sb
                        for ic in range(8):
                            for oc in range(2):
                                occ = gp * 2 + oc
                                def t_kv(e, wsb=wsb, ic=ic, occ=occ, oc=oc, ps_i=ps_i,
                                         wsel=wsel, xbv=xbv):
                                    return e.matmul(psS[ps_i][:, oc * TT:(oc + 1) * TT],
                                                    wsb[:, ic, occ * 128:(occ + 1) * 128],
                                                    xbv[:, ic, :],
                                                    start=(wsel == 0 and ic == 0),
                                                    stop=(wsel == 1 and ic == 7))
                                w = ()
                                if wsel == 0 and ic == 0 and oc == 0:
                                    w = tmax(t_xn, t_wk, t_wv, ps_free[ps_i], t_kvt_free)
                                tk = pl.op("pe", t_kv, waits=w,
                                           inc=(wsel == 1 and ic == 7 and oc == 1))
                    def t_ekv(e, gp=gp, ps_i=ps_i):
                        return e.tensor_copy(kvt[:, 2 * gp:2 * gp + 2, :], psv[ps_i])
                    ps_free[ps_i] = pl.op("dve", t_ekv, waits=tmax(tk), inc=True)
                t_kvt = [ps_free[0], ps_free[1], ps_free[2]]
                ps_i = 0
                for ic in range(8):
                    for oc in range(2):
                        def t_lt(e, ic=ic, oc=oc, ps_i=ps_i):
                            return e.matmul(psS[ps_i][:, oc * TT:(oc + 1) * TT],
                                            wd_sb[:, ic, oc * 128:(oc + 1) * 128],
                                            kvt[:, ic, :],
                                            start=(ic == 0), stop=(ic == 7))
                        w = tmax(tmax(*t_kvt), t_wd, t_lat_free) \
                            if (ic == 0 and oc == 0) else ()
                        tk = pl.op("pe", t_lt, waits=w, inc=(ic == 7 and oc == 1))
                t_kvt_free = tk
                t_latv = pl.op("dve", (lambda e, ps_i=ps_i:
                                       e.tensor_copy(lat[:, :, :], psv[ps_i])),
                               waits=tmax(tk, t_lat_free), inc=True)
                ps_free[0] = t_latv
                t_lat_free = pl.dma("dsp", ag1inl[:, :, tsl], lat[:, :, :],
                                    waits=tmax(t_latv), chan="ag")
            pl.barrier()

        if dump is not None and dump[0] == "ag1in":
            r = _dump_and_stop(ag1in.ap())
            if r:
                return r

        pl.collective("AllGather", ALU.bypass, groups, [ag1in.ap().opt()],
                      [ag1out.ap().opt()])
        pl.barrier()

        with ExitStack() as st4:
            E4 = st4.enter_context
            qup = E4(nc.sbuf_tensor("qup", [128, 2, S], BF16))
            qdn = E4(nc.sbuf_tensor("qdn", [128, 2, S], BF16))
            lup = E4(nc.sbuf_tensor("lup", [128, 2, S], BF16))
            khat = E4(nc.sbuf_tensor("khat", [128, 2, S], BF16))
            vpair = E4(nc.sbuf_tensor("vpair", [128, NKC * 256], BF16))
            pbuf = [E4(nc.sbuf_tensor(f"pbuf{i}", [128, 1024], BF16)) for i in range(R3)]
            rcpb = E4(nc.sbuf_tensor("rcpb", [128, 1024], F32))
            astage = E4(nc.sbuf_tensor("astage", [128, 1024], BF16))
            wq_sb = E4(nc.sbuf_tensor("wq_sb", [128, 8, 256], BF16))
            wu_sb = E4(nc.sbuf_tensor("wu_sb", [128, 2, 512], BF16))
            rq = [E4(nc.sbuf_tensor(f"rq{i}", [128, TT], BF16)) for i in range(R3)]
            vpv = vpair[:, :].rearrange("p (n c) -> p n c", c=256)

            wstg4 = [E4(nc.sbuf_tensor(f"wstg4_{i}", [128, 256], F32)) for i in range(2)]
            t_wq = emit_wload(pl, wstg4, wq, wq_sb, 8, 256)
            t_wuk = emit_wload(pl, wstg4, wuk, wu_sb, 2, 256, dest_off=0)
            t_wuv = emit_wload(pl, wstg4, wuv, wu_sb, 2, 256, dest_off=256)

            t_lat = [pl.dma("dsp", lup[:, :, r * TPC:(r + 1) * TPC], ag1v[r, :, 8:10, :],
                             chan="lat") for r in range(4)]
            lsc_bc = latsc[:, :].unsqueeze(1).unsqueeze(1) \
                .broadcast_to([128, 2, S // 128, 128])
            lupv = lup[:, :, :].rearrange("p i (n c) -> p i n c", c=128)
            t_lup = pl.op("dve", lambda e: e.tensor_tensor(lupv, lupv, lsc_bc, ALU.mult),
                          waits=tmax(tmax(*t_lat), tmax(t_con)), inc=True)

            ps_free = [None] * R3
            rq_free = [None] * R3
            gi = 0
            for r in range(4):
                for tch in range(4):
                    tsl_g = slice(r * TPC + tch * TT, r * TPC + (tch + 1) * TT)
                    tsl_l = slice(tch * TT, (tch + 1) * TT)
                    ps_i = gi % R3
                    for ic in range(8):
                        rri = (gi * 8 + ic) % R3
                        t_rl = pl.dma("dsp", rq[rri][:, :], ag1v[r, :, ic, tsl_l],
                                      waits=tmax(rq_free[rri]), chan="q" + str(rri))
                        for oc in range(2):
                            def t_q(e, rri=rri, ic=ic, oc=oc, ps_i=ps_i):
                                return e.matmul(psS[ps_i][:, oc * TT:(oc + 1) * TT],
                                                wq_sb[:, ic, oc * 128:(oc + 1) * 128],
                                                rq[rri][:, :],
                                                start=(ic == 0), stop=(ic == 7))
                            w = tmax(t_rl, t_wq if gi == 0 and ic == 0 else None,
                                     ps_free[ps_i] if ic == 0 and oc == 0 else None)
                            tk = pl.op("pe", t_q, waits=w, inc=(oc == 1))
                        rq_free[rri] = tk
                    def t_eq(e, tsl_g=tsl_g, ps_i=ps_i):
                        return e.tensor_copy(qdn[:, :, tsl_g], psv[ps_i])
                    ps_free[ps_i] = pl.op("dve", t_eq, waits=tmax(tk), inc=True)
                    gi += 1
            t_qraw = [ps_free[0], ps_free[1], ps_free[2]]
            qup_bc = qsc[:, QT:2 * QT].unsqueeze(1).unsqueeze(1) \
                .broadcast_to([128, 2, S // QT, QT])
            qdn_bc = qsc[:, 0:QT].unsqueeze(1).unsqueeze(1) \
                .broadcast_to([128, 2, S // QT, QT])
            qupv = qup[:, :, :].rearrange("p i (n c) -> p i n c", c=QT)
            qdnv = qdn[:, :, :].rearrange("p i (n c) -> p i n c", c=QT)
            t_qup = pl.op("dve", lambda e: e.tensor_tensor(qupv, qdnv, qup_bc, ALU.mult),
                          waits=tmax(t_qraw, tmax(t_con)), inc=True)
            t_qdn = pl.op("dve", lambda e: e.tensor_tensor(qdnv, qdnv, qdn_bc, ALU.mult),
                          inc=True)

            ps_free2 = [None] * R3
            for tch in range(16):
                tsl_g = slice(tch * TT, (tch + 1) * TT)
                ps_i = tch % R3
                for ic in range(2):
                    for oc in range(2):
                        def t_k(e, ic=ic, oc=oc, ps_i=ps_i, tsl_g=tsl_g):
                            return e.matmul(psS[ps_i][:, oc * TT:(oc + 1) * TT],
                                            wu_sb[:, ic, oc * 128:(oc + 1) * 128],
                                            lup[:, ic, tsl_g],
                                            start=(ic == 0), stop=(ic == 1))
                        w = ()
                        if ic == 0 and oc == 0:
                            w = tmax(t_lup, t_wuk, ps_free2[ps_i], ps_free[ps_i], t_qdn)
                        tk = pl.op("pe", t_k, waits=w, inc=(ic == 1 and oc == 1))
                def t_ek(e, tsl_g=tsl_g, ps_i=ps_i):
                    return e.tensor_copy(khat[:, :, tsl_g], psv[ps_i])
                ps_free2[ps_i] = pl.op("dve", t_ek, waits=tmax(tk), inc=True)
            t_khat = [ps_free2[0], ps_free2[1], ps_free2[2]]
            pl.barrier()
            if dump is not None and dump[0] in ("qup", "qdn", "khat", "lup"):
                src = {"qup": qup, "qdn": qdn, "khat": khat, "lup": lup}[dump[0]]
                pl.dma("dsp", dbg.rearrange("(i p) t -> p i t", p=128), src[:, :, :], chan="dbg")
                pl.barrier()
                pl.emit(block)
                return nc, pl.nops

            vpv4 = vpair[:, :].rearrange("p (n g c) -> p n g c", c=64, g=4)
            t_vones = pl.op("dve", lambda e: e.memset(vpv4[:, :, 1:4:2, :], 1.0), inc=True)

            t_pair_fence = None
            n_pairs, n_qts = 2, NQT
            if dump is not None and dump[0].startswith("atmy_p"):
                n_pairs = int(dump[0][6])
                n_qts = int(dump[0][8:])
            for pair in range(n_pairs):
                vg_tok = None
                for kc in range(NKC):
                    ksl = slice(kc * 128, (kc + 1) * 128)
                    for ic in range(2):
                        def t_vg(e, ic=ic, ksl=ksl, pair=pair):
                            return e.matmul(psO[:, 0:128], lup[:, ic, ksl],
                                            wu_sb[:, ic, 256 + pair * 128:
                                                  256 + (pair + 1) * 128],
                                            start=(ic == 0), stop=(ic == 1))
                        w = ()
                        if ic == 0:
                            w = tmax(vg_tok,
                                     tmax(t_lup, t_wuv, t_pair_fence, t_vones)
                                     if kc == 0 else None)
                        tk = pl.op("pe", t_vg, waits=w, inc=(ic == 1))
                    def t_ev(e, kc=kc):
                        dst = vpair[:, kc * 256:kc * 256 + 256].rearrange(
                            "p (g c) -> p g c", c=64)[:, 0:3:2, :]
                        src = psO[:, 0:128].rearrange("p (g c) -> p g c", c=64)
                        return e.tensor_scalar(dst, src, vundo[:, :], None, ALU.mult)
                    vg_tok = pl.op("dve", t_ev, waits=tmax(tk), inc=True)
                t_vgen = vg_tok

                sfree = [None] * R3
                pfree = [None] * R3
                t_psO_free = t_vgen
                t_ast_store = None
                it = 0
                for qt in range(n_qts):
                    qsl = slice(qt * QT, (qt + 1) * QT)
                    last_av = None
                    for kc in range(NKC):
                        i3 = it % R3
                        ksl = slice(kc * 128, (kc + 1) * 128)
                        delta = qt * 4 - kc
                        if delta >= 1:
                            case, qsrc = "L", qdn
                        elif delta <= -4:
                            case, qsrc = "U", qup
                        else:
                            case, qsrc = "M", qdn
                        w0 = tmax(sfree[i3],
                                  (tmax(t_vgen, t_qup, t_qdn, t_khat)
                                   if it == 0 else None))
                        for half in range(2):
                            hs0, hs1 = half * 64, half * 64 + 64
                            def t_sc(e, i3=i3, hs0=hs0, hs1=hs1, ksl=ksl, qsl=qsl,
                                     half=half, pair=pair, qsrc=qsrc):
                                return e.matmul(psS[i3][:, half * QT:(half + 1) * QT],
                                                khat[hs0:hs1, pair, ksl],
                                                qsrc[hs0:hs1, pair, qsl],
                                                start=True, stop=True,
                                                tile_position=(hs0, 0))
                            tk = pl.op("pe", t_sc, waits=(w0 if half == 0 else ()),
                                       inc=(half == 1))
                        t_mm = tk
                        t_pre = t_mm
                        if case == "M":
                            dpr = kc - qt * 4
                            c_bc = corr[:, dpr * QT:(dpr + 1) * QT].unsqueeze(1) \
                                .broadcast_to([128, 2, QT])
                            def t_co(e, i3=i3, c_bc=c_bc):
                                return e.tensor_tensor(psv[i3], psv[i3], c_bc, ALU.mult)
                            t_pre = pl.op("dve", t_co, waits=tmax(t_mm), inc=True)
                        pw = tmax(t_pre, pfree[i3])
                        if case == "U":
                            m = -delta
                            def t_ex(e, i3=i3, m=m):
                                return e.activation(pbuf[i3][:, :], psS[i3][:, :], AF.Exp,
                                                    bias=0.0, scale=svec[:, m:m + 1])
                        elif case == "L":
                            def t_ex(e, i3=i3, m=delta):
                                return e.activation(pbuf[i3][:, :], psS[i3][:, :], AF.Exp,
                                                    bias=0.0, scale=ldsc[:, m:m + 1])
                        else:
                            def t_ex(e, i3=i3):
                                return e.activation(pbuf[i3][:, :], psS[i3][:, :], AF.Exp,
                                                    bias=0.0, scale=lnmisc[:, 2:3])
                        t_exp = pl.op("act", t_ex, waits=pw, inc=True)
                        sfree[i3] = t_exp
                        for half in range(2):
                            base = kc * 256 + half * 128
                            lv = vpair[:, base:base + 128]
                            def t_av(e, lv=lv, i3=i3, half=half, kc=kc):
                                return e.matmul(psO[:, half * QT:(half + 1) * QT], lv,
                                                pbuf[i3][:, half * QT:(half + 1) * QT],
                                                start=(kc == 0), stop=(kc == NKC - 1))
                            w = tmax(t_exp,
                                     t_psO_free if (kc == 0 and half == 0) else None)
                            tk = pl.op("pe", t_av, waits=w, inc=(half == 1))
                        pfree[i3] = tk
                        last_av = tk
                        it += 1
                    ni = it % R3
                    t_rc = pl.op("dve",
                                 lambda e: e.reciprocal(rcpb[64:128, :], psO[64:128, :]),
                                 waits=tmax(last_av), inc=True)
                    for half in range(2):
                        def t_nb(e, half=half, ni=ni):
                            return e.matmul(psS[ni][0:64, half * QT:(half + 1) * QT],
                                            m64[64:128, :],
                                            rcpb[64:128, half * QT:(half + 1) * QT],
                                            start=True, stop=True,
                                            tile_position=(64, 0))
                        tk = pl.op("pe", t_nb,
                                   waits=tmax(t_rc, sfree[ni]) if half == 0 else (),
                                   inc=(half == 1))
                    t_rcb = pl.op("dve",
                                  (lambda e, ni=ni:
                                   e.tensor_copy(rcpb[0:64, :], psS[ni][0:64, :])),
                                  waits=tmax(tk), inc=True)
                    sfree[ni] = t_rcb
                    t_norm = pl.op("dve",
                                   lambda e: e.tensor_tensor(astage[0:64, :],
                                                             psO[0:64, :],
                                                             rcpb[0:64, :], ALU.mult),
                                   waits=tmax(t_rcb, t_ast_store), inc=True)
                    t_psO_free = t_norm
                    st_t = []
                    for half in range(2):
                        hrow = (2 * pair + half) * 64
                        st_t.append(pl.dma(
                            "dsp", atmy_dram.ap()[hrow:hrow + 64, qsl],
                            astage[0:64, half * QT:(half + 1) * QT],
                            waits=tmax(t_norm), chan="as"))
                    t_ast_store = tmax(*st_t)
                t_pair_fence = [t_norm, last_av]
            pl.barrier()

        if dump is not None and (dump[0] == "atmy" or dump[0].startswith("atmy_p")):
            r = _dump_and_stop(atmy_dram.ap())
            if r:
                return r

        with ExitStack() as st6:
            E6 = st6.enter_context
            at_sb = E6(nc.sbuf_tensor("at_sb", [128, 2, S], BF16))
            wo_sb = E6(nc.sbuf_tensor("wo_sb", [128, 2, D], BF16))
            rstage = [E6(nc.sbuf_tensor(f"rstage{i}", [128, 1024], BF16)) for i in range(2)]
            wstg6 = [E6(nc.sbuf_tensor(f"wstg6_{i}", [128, D], F32)) for i in range(2)]
            t_wo = emit_wload(pl, wstg6, wo, wo_sb, 2, D)
            t_at = pl.dma("dsp", at_sb[:, :, :], atmyv, chan="at")
            ps_free = [None] * R3
            rs_free = [None, None]
            for tch in range(S // 128):
                ps_i = tch % R3
                bi = tch % 2
                for half in range(2):
                    for ic in range(2):
                        def t_wp(e, ic=ic, half=half, ps_i=ps_i, tch=tch):
                            return e.matmul(psS[ps_i][:, half * TT:(half + 1) * TT],
                                            at_sb[:, ic, tch * 128:(tch + 1) * 128],
                                            wo_sb[:, ic, half * TT:(half + 1) * TT],
                                            start=(ic == 0), stop=(ic == 1))
                        w = tmax(t_at, t_wo, ps_free[ps_i]) \
                            if (half == 0 and ic == 0) else ()
                        tk = pl.op("pe", t_wp, waits=w, inc=(half == 1 and ic == 1))
                def t_er(e, bi=bi, ps_i=ps_i):
                    return e.tensor_copy(rstage[bi][:, :], psS[ps_i][:, :])
                t_e = pl.op("dve", t_er, waits=tmax(tk, rs_free[bi]), inc=True)
                ps_free[ps_i] = t_e
                rs_free[bi] = pl.dma("dsp", rsin.ap()[tch * 128:(tch + 1) * 128, :],
                                     rstage[bi][:, :], waits=tmax(t_e), chan="rs")
            pl.barrier()
            pl.collective("ReduceScatter", ALU.add, groups, [rsin.ap().opt()],
                          [rsout.ap().opt()])
            pl.barrier()

        if dump is not None and dump[0] == "rsout":
            r = _dump_and_stop(rsout.ap())
            if r:
                return r

        with ExitStack() as st7:
            E7 = st7.enter_context
            rt = [E7(nc.sbuf_tensor(f"rt{i}", [128, 1024], F32)) for i in range(2)]
            rtb = [E7(nc.sbuf_tensor(f"rtb{i}", [128, 1024], BF16)) for i in range(2)]
            xtile = [E7(nc.sbuf_tensor(f"xtile{i}", [128, 8, 128], F32)) for i in range(2)]
            rt_free = [None, None]
            rtb_free = [None, None]
            xt_free = [None, None]
            tens4 = psS + [psO]
            t_evprev = None
            for tch in range(TPC // 128):
                bi = tch % 2
                t_rb = pl.dma("dsp", rtb[bi][:, :], rsout.ap()[tch * 128:(tch + 1) * 128, :],
                              waits=tmax(rtb_free[bi]), chan="rt")
                def t_rcv(e, bi=bi):
                    return e.tensor_copy(rt[bi][:, :], rtb[bi][:, :])
                t_r = pl.op("dve", t_rcv, waits=tmax(t_rb, rt_free[bi]), inc=True)
                rtb_free[bi] = t_r
                t_xl = pl.dma("dsp", xtile[bi][:, :, :],
                              xfm[:, :, tch * 128:(tch + 1) * 128],
                              waits=tmax(xt_free[bi]), chan="xl")
                for ic in range(8):
                    tns = tens4[ic // 2]
                    co = (ic % 2) * 512
                    def t_tr(e, ic=ic, bi=bi, tns=tns, co=co):
                        return e.transpose(tns[:, co:co + 128],
                                           rt[bi][:, ic * 128:(ic + 1) * 128], id_f[:, :])
                    w = tmax(t_r, t_evprev) if ic == 0 else ()
                    tk = pl.op("pe", t_tr, waits=w, inc=(ic == 7))
                rt_free[bi] = tk
                for j in range(4):
                    def t_add(e, bi=bi, j=j, tns=tens4[j]):
                        pv = tns[:, :].rearrange("p (s c) -> p s c", s=2)[:, :, 0:128]
                        xv = xtile[bi][:, 2 * j:2 * j + 2, :]
                        return e.tensor_tensor(xv, xv, pv, ALU.add)
                    t_a = pl.op("dve", t_add,
                                waits=tmax(tk, t_xl) if j == 0 else (), inc=(j == 3))
                t_evprev = t_a
                xt_free[bi] = pl.dma("dsp", xfm[:, :, tch * 128:(tch + 1) * 128],
                                     xtile[bi][:, :, :], waits=tmax(t_a), chan="xs")
            pl.barrier()

        if dump is not None and dump[0] == "x2":
            r = _dump_and_stop(x_dram.ap())
            if r:
                return r

        emit_ffn()
        if dump is not None and dump[0] == "x3":
            r = _dump_and_stop(x_dram.ap())
            if r:
                return r

        with ExitStack() as st9:
            E9 = st9.enter_context
            t9in = [E9(nc.sbuf_tensor(f"oin{i}", [128, 8, 128], F32)) for i in range(2)]
            t9out = [E9(nc.sbuf_tensor(f"oout{i}", [128, 1024], F32)) for i in range(2)]
            t9in_tok = [None, None]
            t9out_tok = [None, None]
            tens4 = psS + [psO]
            t_evprev = None
            for tch in range(TPC // 128):
                bi = tch % 2
                t_ld = pl.dma("dsp", t9in[bi][:, :, :], xfm[:, :, tch * 128:(tch + 1) * 128],
                              waits=tmax(t9in_tok[bi]), chan="xl")
                for ic in range(8):
                    tns = tens4[ic // 2]
                    co = (ic % 2) * 512
                    def t_tr(e, ic=ic, bi=bi, tns=tns, co=co):
                        return e.transpose(tns[:, co:co + 128],
                                           t9in[bi][:, ic, :], id_f[:, :])
                    w = tmax(t_ld, t_evprev) if ic == 0 else ()
                    tk = pl.op("pe", t_tr, waits=w, inc=(ic == 7))
                t9in_tok[bi] = tk
                for j in range(4):
                    def t_ev(e, bi=bi, j=j, tns=tens4[j]):
                        src = tns[:, :].rearrange("p (s c) -> p s c", s=2)[:, :, 0:128]
                        dst = t9out[bi][:, j * 256:(j + 1) * 256].rearrange(
                            "p (s c) -> p s c", c=128)
                        return e.tensor_copy(dst, src)
                    t_e = pl.op("dve", t_ev,
                                waits=tmax(tk, t9out_tok[bi]) if j == 0 else (),
                                inc=(j == 3))
                t_evprev = t_e
                t9out_tok[bi] = pl.dma("dsp", y[tch * 128:(tch + 1) * 128, :],
                                      t9out[bi][:, :], waits=tmax(t_e), chan="out")
            pl.barrier()

        pl.emit(block)
    return nc, pl.nops


def _consts():
    lam = LAM
    u = np.arange(QT, dtype=np.float64)
    qsc_dn = np.exp(-lam * u) / 32.0
    qsc_up = np.exp(lam * u) / 32.0
    qsc = np.broadcast_to(np.concatenate([qsc_dn, qsc_up])[None, :], (128, 2 * QT))
    latsc = np.broadcast_to(np.exp(lam * np.arange(128, dtype=np.float64))[None, :],
                            (128, 128))
    vundo = np.exp(-lam * np.arange(128, dtype=np.float64))[:, None]
    vv = np.arange(128, dtype=np.float64)[:, None]
    uu = np.arange(QT, dtype=np.float64)[None, :]
    tiles = []
    for dp in (0, 128, 256, 384):
        w = uu - vv
        c = np.where(w >= dp, np.exp(lam * dp), np.exp(2 * lam * w - lam * dp))
        tiles.append(c)
    corr = np.concatenate(tiles, axis=1)
    mm_ = np.arange(64, dtype=np.float64)[None, :]
    svec = np.exp(-lam * 128.0 * mm_ - 2.0 * lam * vv)
    ldsc = np.broadcast_to(np.exp(-lam * 128.0 * np.arange(64, dtype=np.float64))[None, :],
                           (128, 64))
    return (np.ascontiguousarray(qsc, dtype=np.float32),
            np.ascontiguousarray(latsc, dtype=np.float32),
            np.ascontiguousarray(vundo, dtype=np.float32),
            np.ascontiguousarray(corr, dtype=np.float32),
            np.ascontiguousarray(svec, dtype=np.float32),
            np.ascontiguousarray(ldsc, dtype=np.float32))


def make_in_maps(inputs):
    x = np.asarray(inputs["L0_attn_x"], dtype=np.float32)
    qsc, latsc, vundo, corr, svec, ldsc = _consts()
    ident = np.eye(128, dtype=np.float32)
    com = {
        "wk": np.ascontiguousarray(inputs["L0_attn_W_k"], dtype=np.float32),
        "wv": np.ascontiguousarray(inputs["L0_attn_W_v"], dtype=np.float32),
        "wd": np.ascontiguousarray(inputs["L0_attn_W_down_kv"], dtype=np.float32),
        "k0u": np.ascontiguousarray(np.asarray(inputs["L0_ffn_k0"]).view(np.uint8)),
        "k1u": np.ascontiguousarray(np.asarray(inputs["L0_ffn_k1"]).view(np.uint8)),
        "k2u": np.ascontiguousarray(np.asarray(inputs["L0_ffn_k2"]).view(np.uint8)),
        "s0in": np.ascontiguousarray(
            np.broadcast_to(np.asarray(inputs["L0_ffn_s0"], dtype=np.float32), (128, 1))),
        "s1in": np.ascontiguousarray(
            np.broadcast_to(np.asarray(inputs["L0_ffn_s1"], dtype=np.float32), (128, 1))),
        "s2in": np.ascontiguousarray(
            np.broadcast_to(np.asarray(inputs["L0_ffn_s2"], dtype=np.float32), (128, 1))),
        "ident": ident, "qsc_i": qsc, "latsc_i": latsc, "vundo_i": vundo,
        "corr_i": corr, "svec_i": svec, "ldsc_i": ldsc,
    }
    wq_full = np.asarray(inputs["L0_attn_W_q"], dtype=np.float32)
    wo_full = np.asarray(inputs["L0_attn_W_o"], dtype=np.float32)
    wuk_full = np.asarray(inputs["L0_attn_W_up_k"], dtype=np.float32)
    wuv_full = np.asarray(inputs["L0_attn_W_up_v"], dtype=np.float32)
    in_maps = []
    for c in range(8):
        b, r = c // 4, c % 4
        m = dict(com)
        m["xin"] = np.ascontiguousarray(x[b, r * TPC:(r + 1) * TPC, :])
        m["wq"] = np.ascontiguousarray(wq_full[:, r * 256:(r + 1) * 256])
        m["wo"] = np.ascontiguousarray(wo_full[r * 256:(r + 1) * 256, :])
        m["wuk"] = np.ascontiguousarray(wuk_full[:, r * 256:(r + 1) * 256])
        m["wuv"] = np.ascontiguousarray(wuv_full[:, r * 256:(r + 1) * 256])
        in_maps.append(m)
    return in_maps


def kernel(**inputs):
    in_maps = make_in_maps(inputs)
    nc, _ = build_nc()
    res = run_bass_kernel_spmd(nc, in_maps, core_ids=list(range(8)))
    out = np.empty((B, S, D), dtype=np.float32)
    for c in range(8):
        b, r = c // 4, c % 4
        out[b, r * TPC:(r + 1) * TPC, :] = res.results[c]["y"]
    return out


# version 02 (final)
# speedup vs baseline: 1.0320x; 1.0320x over previous
import math
from contextlib import ExitStack


class _StopBuild(Exception):
    pass

import numpy as np
import ml_dtypes

import concourse.bass as bass
import concourse.mybir as mybir
from concourse.bass_utils import run_bass_kernel_spmd

F32 = mybir.dt.float32
BF16 = mybir.dt.bfloat16
U8 = mybir.dt.uint8
AF = mybir.ActivationFunctionType
ALU = mybir.AluOpType

B, S, D = 2, 8192, 1024
TPC = 2048
H, HD, DL = 16, 64, 256
LAM = 0.01
DF = 4096
TT = 512
NTT = TPC // TT
EPS = 1e-5
QT = 512
NQT = S // QT
NKC = S // 128
R3 = 3
DEBUG_SERIAL = [0]
DEBUG_SKIP = set()
FLUSH_INCS = [False]

ENGS = ("pe", "act", "dve", "pool", "sp")
DQS = ("dsp", "dact", "dpool")


class Plan:
    def __init__(self, nc, stack):
        self.nc = nc
        self.stack = stack
        self.ops = {e: [] for e in ENGS}
        self.sem = {}
        for e in ENGS:
            self.sem[e] = stack.enter_context(nc.semaphore("s_" + e))
        self.sem["cc"] = stack.enter_context(nc.semaphore("s_cc"))
        self.cnt = {k: 0 for k in list(ENGS) + ["cc"]}
        self.chans = set()
        self.nops = 0
        self.flush_thunk = {}

    def chan(self, name):
        if name not in self.chans:
            self.sem[name] = self.stack.enter_context(self.nc.semaphore("c_" + name))
            self.cnt[name] = 0
            self.chans.add(name)
        return name

    def set_flush(self, fl_sb):
        self.flush_thunk = {
            "dve": (lambda e: e.tensor_copy(fl_sb[0:1, 0:1], fl_sb[0:1, 1:2])),
            "act": (lambda e: e.copy(fl_sb[0:1, 2:3], fl_sb[0:1, 3:4])),
        }

    def op(self, eng, thunk, waits=(), inc=False):
        ws = tuple((k, v) for (k, v) in waits if v > 0 and k != eng)
        if inc and FLUSH_INCS[0] and eng in self.flush_thunk:
            self.ops[eng].append((ws, thunk, None))
            self.cnt[eng] += 1
            self.ops[eng].append(((), self.flush_thunk[eng], (eng, 1)))
            self.nops += 2
            return (eng, self.cnt[eng])
        incspec = None
        if inc:
            self.cnt[eng] += 1
            incspec = (eng, 1)
        self.ops[eng].append((ws, thunk, incspec))
        self.nops += 1
        return (eng, self.cnt[eng])

    def dma(self, q, out, in_, waits=(), chan="misc", **kw):
        eng = {"dsp": "sp", "dact": "act", "dpool": "pool"}[q]
        c = self.chan(chan)
        ws = tuple((k, v) for (k, v) in list(waits) + [(c, self.cnt[c])] if v > 0)
        self.cnt[c] += 16

        def t(e, out=out, in_=in_, kw=kw):
            return e.dma_start(out=out, in_=in_, **kw)

        self.ops[eng].append((ws, t, (c, 16)))
        self.nops += 1
        return (c, self.cnt[c])

    def collective(self, kind, op_, groups, ins, outs, waits=()):
        ws = tuple((k, v) for (k, v) in waits if v > 0)
        self.cnt["cc"] += 1

        def t(e, kind=kind, op_=op_, groups=groups, ins=ins, outs=outs):
            return e.collective_compute(kind, op_, replica_groups=groups, ins=ins, outs=outs)

        self.ops["pool"].append((ws, t, ("cc", 1)))
        self.nops += 1
        return ("cc", self.cnt["cc"])

    def barrier(self):
        for e in ENGS:
            for j in range(len(self.ops[e]) - 1, -1, -1):
                ws_j, th_j, inc_j = self.ops[e][j]
                if th_j is None:
                    continue
                if inc_j is not None and inc_j[0] in self.chans:
                    continue
                if inc_j is None:
                    self.cnt[e] += 1
                    self.ops[e][j] = (ws_j, th_j, (e, 1))
                break
        snap = dict(self.cnt)
        for e in ENGS:
            ws = [(k, snap[k]) for k in ENGS if k != e]
            ws += [(c, snap[c]) for c in self.chans]
            ws += [("cc", snap["cc"])]
            ws = tuple((k, v) for (k, v) in ws if v > 0)
            self.ops[e].append((ws, None, None))
            self.nops += 1

    def emit(self, block):
        def runner(ops):
            def f(eng):
                guaranteed = {}
                for ws, thunk, incspec in ops:
                    for (k, v) in ws:
                        if guaranteed.get(k, -1) >= v:
                            continue
                        eng.wait_ge(self.sem[k], v)
                        guaranteed[k] = v
                    if thunk is None:
                        continue
                    inst = thunk(eng)
                    if incspec is not None:
                        inst.then_inc(self.sem[incspec[0]], incspec[1])
            return f
        block.tensor(runner(self.ops["pe"]))
        block.scalar(runner(self.ops["act"]))
        block.vector(runner(self.ops["dve"]))
        block.gpsimd(runner(self.ops["pool"]))
        block.sync(runner(self.ops["sp"]))


def tmax(*toks):
    d = {}
    for t in toks:
        if t is None:
            continue
        items = t if isinstance(t, list) else [t]
        for it_ in items:
            if it_ is None:
                continue
            k, v = it_
            d[k] = max(d.get(k, 0), v)
    return list(d.items())


def build_nc(dump=None):
    nc = bass.Bass("TRN2", target_bir_lowering=False)

    xin = nc.dram_tensor("xin", [TPC, D], F32, kind="ExternalInput").ap()
    wq = nc.dram_tensor("wq", [D, 256], F32, kind="ExternalInput").ap()
    wk = nc.dram_tensor("wk", [D, D], F32, kind="ExternalInput").ap()
    wv = nc.dram_tensor("wv", [D, D], F32, kind="ExternalInput").ap()
    wo = nc.dram_tensor("wo", [256, D], F32, kind="ExternalInput").ap()
    wd = nc.dram_tensor("wd", [D, DL], F32, kind="ExternalInput").ap()
    wuk = nc.dram_tensor("wuk", [DL, 256], F32, kind="ExternalInput").ap()
    wuv = nc.dram_tensor("wuv", [DL, 256], F32, kind="ExternalInput").ap()
    k0u = nc.dram_tensor("k0u", [2 * D, DF], U8, kind="ExternalInput").ap()
    k1u = nc.dram_tensor("k1u", [DF, DF], U8, kind="ExternalInput").ap()
    k2u = nc.dram_tensor("k2u", [DF, D], U8, kind="ExternalInput").ap()
    s0in = nc.dram_tensor("s0in", [128, 1], F32, kind="ExternalInput").ap()
    s1in = nc.dram_tensor("s1in", [128, 1], F32, kind="ExternalInput").ap()
    s2in = nc.dram_tensor("s2in", [128, 1], F32, kind="ExternalInput").ap()
    ident_in = nc.dram_tensor("ident", [128, 128], F32, kind="ExternalInput").ap()
    qsc_in = nc.dram_tensor("qsc_i", [128, 2 * QT], F32, kind="ExternalInput").ap()
    latsc_in = nc.dram_tensor("latsc_i", [128, 128], F32, kind="ExternalInput").ap()
    vundo_in = nc.dram_tensor("vundo_i", [128, 1], F32, kind="ExternalInput").ap()
    corr_in = nc.dram_tensor("corr_i", [128, 4 * QT], F32, kind="ExternalInput").ap()
    svec_in = nc.dram_tensor("svec_i", [128, 64], F32, kind="ExternalInput").ap()
    ldsc_in = nc.dram_tensor("ldsc_i", [128, 64], F32, kind="ExternalInput").ap()
    y = nc.dram_tensor("y", [TPC, D], F32, kind="ExternalOutput").ap()

    x_dram = nc.dram_tensor("x_dram", [D, TPC], F32)
    k0f_dram = nc.dram_tensor("k0f_dram", [D, DF], BF16)
    w1_dram = nc.dram_tensor("w1_dram", [DF, DF], BF16)
    w2_dram = nc.dram_tensor("w2_dram", [DF, D], BF16)
    ag1in = nc.dram_tensor("ag1in", [1280, TPC], BF16)
    ag1out = nc.dram_tensor("ag1out", [4 * 1280, TPC], BF16)
    atmy_dram = nc.dram_tensor("atmy_dram", [256, S], BF16)
    rsin = nc.dram_tensor("rsin", [S, D], BF16)
    rsout = nc.dram_tensor("rsout", [TPC, D], BF16)
    groups = [[0, 1, 2, 3], [4, 5, 6, 7]]
    dbg = None
    if dump is not None:
        dbg = nc.dram_tensor("dbg", list(dump[1]),
                             dump[2] if len(dump) > 2 else F32, kind="ExternalOutput").ap()

    xfm = x_dram.ap().rearrange("(i p) t -> p i t", p=128)
    k0fv = k0f_dram.ap().rearrange("(i p) c -> p i c", p=128)
    w1v = w1_dram.ap().rearrange("(i p) c -> p i c", p=128)
    w2v = w2_dram.ap().rearrange("(i p) c -> p i c", p=128)
    ag1inx = ag1in.ap()[0:D, :].rearrange("(i p) t -> p i t", p=128)
    ag1inl = ag1in.ap()[D:1280, :].rearrange("(i p) t -> p i t", p=128)
    ag1v = ag1out.ap().rearrange("(r i p) t -> r p i t", r=4, p=128)
    atmyv = atmy_dram.ap().rearrange("(i p) t -> p i t", p=128)

    with ExitStack() as st0:
        E = st0.enter_context
        pl = Plan(nc, st0)
        block = E(nc.Block())
        fl_sb = E(nc.sbuf_tensor("fl_sb", [1, 4], F32))
        pl.set_flush(fl_sb)
        pl.op("dve", lambda e: e.memset(fl_sb[0:1, :], 0.0))

        id_f = E(nc.sbuf_tensor("id_f", [128, 128], F32))
        ones_bf = E(nc.sbuf_tensor("ones_bf", [128, 128], BF16))
        m64 = E(nc.sbuf_tensor("m64", [128, 64], F32))
        qsc = E(nc.sbuf_tensor("qsc", [128, 2 * QT], F32))
        latsc = E(nc.sbuf_tensor("latsc", [128, 128], F32))
        vundo = E(nc.sbuf_tensor("vundo", [128, 1], F32))
        corr = E(nc.sbuf_tensor("corr", [128, 4 * QT], F32))
        svec = E(nc.sbuf_tensor("svec", [128, 64], F32))
        ldsc = E(nc.sbuf_tensor("ldsc", [128, 64], F32))
        lnmisc = E(nc.sbuf_tensor("lnmisc", [128, 3], F32))
        s0x4 = E(nc.sbuf_tensor("s0x4", [128, 1], F32))
        s1x2 = E(nc.sbuf_tensor("s1x2", [128, 1], F32))
        s2h = E(nc.sbuf_tensor("s2h", [128, 1], F32))
        psS = [E(nc.psum_tensor(f"psS{i}", [128, 1024], F32)) for i in range(R3)]
        psO = E(nc.psum_tensor("psO", [128, 1024], F32))
        psv = [p[:, :].rearrange("p (o t) -> p o t", o=2) for p in psS]

        t_con = [pl.dma("dsp", id_f[:, :], ident_in, chan="cst"),
                 pl.dma("dsp", qsc[:, :], qsc_in, chan="cst"),
                 pl.dma("dsp", latsc[:, :], latsc_in, chan="cst"),
                 pl.dma("dsp", vundo[:, :], vundo_in, chan="cst"),
                 pl.dma("dsp", corr[:, :], corr_in, chan="cst"),
                 pl.dma("dsp", svec[:, :], svec_in, chan="cst"),
                 pl.dma("dsp", ldsc[:, :], ldsc_in, chan="cst")]
        t_s0 = pl.dma("dsp", s0x4[:, :], s0in, chan="cst")
        t_s1 = pl.dma("dsp", s1x2[:, :], s1in, chan="cst")
        t_s2 = pl.dma("dsp", s2h[:, :], s2in, chan="cst")
        pl.op("dve", lambda e: e.memset(ones_bf[:, :], 1.0), inc=True)
        pl.op("dve", lambda e: e.memset(lnmisc[:, 0:1], EPS), inc=True)
        pl.op("dve", lambda e: e.memset(lnmisc[:, 1:2], -0.5), inc=True)
        pl.op("dve", lambda e: e.memset(lnmisc[:, 2:3], 1.0), inc=True)
        pl.op("dve", lambda e: e.memset(m64[:, :], 1.0 / 64.0), inc=True)
        pl.op("dve", lambda e: e.tensor_scalar(s0x4[:, :], s0x4[:, :], 4.0, None, ALU.mult),
              waits=tmax(t_s0))
        pl.op("dve", lambda e: e.tensor_scalar(s1x2[:, :], s1x2[:, :], 2.0, None, ALU.mult),
              waits=tmax(t_s1))
        t_sv = pl.op("dve", lambda e: e.tensor_scalar(s2h[:, :], s2h[:, :], 0.5, None, ALU.mult),
                     waits=tmax(t_s2), inc=True)

        if dump is not None and dump[0] == "con":
            pl.barrier()
            pl.dma("dsp", dbg, qsc_in, chan="dbg")
            pl.barrier()
            pl.emit(block)
            return nc, pl.nops

        with ExitStack() as sp0:
            Ep = sp0.enter_context
            tin = [Ep(nc.sbuf_tensor(f"tin{i}", [128, D], F32)) for i in range(2)]
            tout = [Ep(nc.sbuf_tensor(f"tout{i}", [128, 1024], F32)) for i in range(2)]
            tin_tok = [None, None]
            tout_tok = [None, None]
            tens4 = psS + [psO]
            t_evprev = None
            t_idc = tmax(t_con)
            for tch in range(0 if "transpose" in DEBUG_SKIP else TPC // 128):
                bi = tch % 2
                t_ld = pl.dma("dsp", tin[bi][:, :], xin[tch * 128:(tch + 1) * 128, :],
                              waits=tmax(tin_tok[bi]), chan="xl")
                for ic in range(8):
                    tns = tens4[ic // 2]
                    co = (ic % 2) * 512
                    def t_tr(e, ic=ic, bi=bi, tns=tns, co=co):
                        return e.transpose(tns[:, co:co + 128],
                                           tin[bi][:, ic * 128:(ic + 1) * 128], id_f[:, :])
                    w = tmax(t_ld, t_idc, t_evprev) if ic == 0 else ()
                    tk = pl.op("pe", t_tr, waits=w, inc=(ic == 7))
                tin_tok[bi] = tk
                for j in range(4):
                    def t_ev(e, bi=bi, j=j, tns=tens4[j]):
                        src = tns[:, :].rearrange("p (s c) -> p s c", s=2)[:, :, 0:128]
                        dst = tout[bi][:, j * 256:(j + 1) * 256].rearrange(
                            "p (s c) -> p s c", c=128)
                        return e.tensor_copy(dst, src)
                    t_e = pl.op("dve", t_ev,
                                waits=tmax(tk, tout_tok[bi]) if j == 0 else (),
                                inc=(j == 3))
                t_evprev = t_e
                tout_tok[bi] = pl.dma(
                    "dsp", xfm[:, :, tch * 128:(tch + 1) * 128],
                    tout[bi][:, :].rearrange("p (i t) -> p i t", t=128), waits=tmax(t_e),
                    chan="xs")
                if DEBUG_SERIAL[0] >= 3:
                    pl.barrier()

            if dump is not None and dump[0] == "x0t":
                pl.barrier()
                pl.dma("dsp", dbg, x_dram.ap(), chan="dbg")
                pl.barrier()
                pl.emit(block)
                return nc, pl.nops
            cu8 = [Ep(nc.sbuf_tensor(f"cu8_{i}", [128, DF], U8)) for i in range(2)]
            cbf = [Ep(nc.sbuf_tensor(f"cbf_{i}", [128, 2048], BF16)) for i in range(2)]
            cu8_tok = [None, None]
            cbf_tok = [None, None]
            conv_i = [0]
            t_k0_done = [None]
            t_w1_done = [None]
            t_w2_done = [None]

            def conv(src1, src2, dst, ncols, fold):
                bi = conv_i[0] % 2
                conv_i[0] += 1
                t_l = pl.dma("dsp", cu8[bi][:, 0:ncols], src1, waits=tmax(cu8_tok[bi]), chan="cvl")
                wl = [t_l]
                if fold:
                    t_l2 = pl.dma("dsp", cu8[bi][:, ncols:2 * ncols], src2, chan="cvl2")
                    wl.append(t_l2)
                    def tc(e, bi=bi, ncols=ncols):
                        return e.tensor_tensor(cbf[bi][:, 0:ncols], cu8[bi][:, 0:ncols],
                                               cu8[bi][:, ncols:2 * ncols], ALU.subtract)
                else:
                    def tc(e, bi=bi, ncols=ncols):
                        return e.tensor_scalar(cbf[bi][:, 0:ncols], cu8[bi][:, 0:ncols],
                                               2.0, -1.0, ALU.mult, ALU.add)
                t_c = pl.op("dve", tc, waits=tmax(*wl, cbf_tok[bi]), inc=True)
                cu8_tok[bi] = t_c
                t_w = pl.dma("dact", dst, cbf[bi][:, 0:ncols], waits=tmax(t_c), chan="cvs")
                cbf_tok[bi] = t_w
                if DEBUG_SERIAL[0] >= 3:
                    pl.barrier()
                return t_w

            conv_ranges = (0, 0, 0) if "conv" in DEBUG_SKIP else (8, 32, 32)
            for i in range(conv_ranges[0]):
                for cy in range(2):
                    csl = slice(cy * 2048, (cy + 1) * 2048)
                    t_k0_done[0] = conv(k0u[i * 128:(i + 1) * 128, csl],
                                        k0u[D + i * 128:D + (i + 1) * 128, csl],
                                        k0f_dram[i * 128:(i + 1) * 128, csl], 2048, True)
            for i in range(conv_ranges[1]):
                for cy in range(2):
                    csl = slice(cy * 2048, (cy + 1) * 2048)
                    t_w1_done[0] = conv(k1u[i * 128:(i + 1) * 128, csl], None,
                                        w1_dram[i * 128:(i + 1) * 128, csl], 2048, False)
            for i in range(conv_ranges[2]):
                t_w2_done[0] = conv(k2u[i * 128:(i + 1) * 128, :], None,
                                    w2_dram[i * 128:(i + 1) * 128, :], D, False)
            pl.barrier()

        def _dump_and_stop2(src_ap):
            pl.barrier()
            pl.dma("dsp", dbg, src_ap, chan="dbg")
            pl.barrier()
            pl.emit(block)
            return nc, pl.nops

        def _dump_and_stop(src_ap):
            pl.barrier()
            pl.dma("dsp", dbg, src_ap, chan="dbg")
            pl.barrier()
            pl.emit(block)
            return nc, pl.nops

        if dump is not None and dump[0] == "k0f":
            r = _dump_and_stop2(k0f_dram.ap())
            if r:
                return r
        if dump is not None and dump[0] == "w1c":
            r = _dump_and_stop2(w1_dram.ap()[0:1024, :])
            if r:
                return r
        if dump is not None and dump[0] == "w1d":
            r = _dump_and_stop2(w1_dram.ap()[3072:4096, :])
            if r:
                return r
        if dump is not None and dump[0] == "w2c":
            r = _dump_and_stop2(w2_dram.ap())
            if r:
                return r
        if dump is not None and dump[0] == "x0":
            r = _dump_and_stop(x_dram.ap())
            if r:
                return r

        ffn_call = [0]

        def emit_ffn():
            fc = ffn_call[0]
            ffn_call[0] += 1
            with ExitStack() as stf:
                Ef = stf.enter_context
                xt = Ef(nc.sbuf_tensor(f"xt_f{fc}", [128, 8, TT], F32))
                xb = Ef(nc.sbuf_tensor(f"xb_f{fc}", [128, 8, TT], BF16))
                xsq = Ef(nc.sbuf_tensor(f"xsq_f{fc}", [128, 8, TT], BF16))
                mrow = Ef(nc.sbuf_tensor(f"mrow_f{fc}", [128, TT], F32))
                msq = Ef(nc.sbuf_tensor(f"msq_f{fc}", [128, TT], F32))
                varr = Ef(nc.sbuf_tensor(f"varr_f{fc}", [128, TT], F32))
                rstd = Ef(nc.sbuf_tensor(f"rstd_f{fc}", [128, TT], F32))
                h1 = Ef(nc.sbuf_tensor(f"h1_f{fc}", [128, 32, TT], BF16))
                h2 = Ef(nc.sbuf_tensor(f"h2_f{fc}", [128, 32, TT], BF16))
                wr = [Ef(nc.sbuf_tensor(f"wr{i}_f{fc}", [128, 8, 256], BF16)) for i in range(R3)]
                wr_free = [None] * R3
                ps_free = [None] * R3
                ring = [0]
                t_stats_free = None
                t_prev_store = None
                t_prev_mm0 = None
                m_bc = mrow[:, :].unsqueeze(1).broadcast_to([128, 8, TT])
                r_bc = rstd[:, :].unsqueeze(1).broadcast_to([128, 8, TT])

                for tt_i in range(NTT):
                    tsl = slice(tt_i * TT, (tt_i + 1) * TT)
                    t_x = pl.dma("dsp", xt[:, :, :], xfm[:, :, tsl],
                                 waits=tmax(t_prev_store,
                                            ps_free[0], ps_free[1], ps_free[2]),
                                 chan="xl")
                    t_xb = pl.op("dve", lambda e: e.tensor_copy(xb[:, :, :], xt[:, :, :]),
                                 waits=tmax(t_x, t_prev_mm0), inc=True)
                    t_sq = pl.op("act", lambda e: e.activation(xsq[:, :, :], xb[:, :, :],
                                                               AF.Square),
                                 waits=tmax(t_xb), inc=True)
                    for ic in range(8):
                        def t_s(e, ic=ic):
                            return e.matmul(psO[:, 0:TT], ones_bf[:, :], xb[:, ic, :],
                                            start=(ic == 0), stop=(ic == 7))
                        pl.op("pe", t_s,
                              waits=tmax(t_xb, t_stats_free) if ic == 0 else ())
                    for ic in range(8):
                        def t_s(e, ic=ic):
                            return e.matmul(psO[:, TT:2 * TT], ones_bf[:, :], xsq[:, ic, :],
                                            start=(ic == 0), stop=(ic == 7))
                        tk = pl.op("pe", t_s, waits=tmax(t_sq) if ic == 0 else (),
                                   inc=(ic == 7))
                    t_m = pl.op("dve", lambda e: e.tensor_scalar(
                        mrow[:, :], psO[:, 0:TT], 1.0 / D, None, ALU.mult),
                        waits=tmax(tk), inc=True)
                    pl.op("dve", lambda e: e.tensor_tensor(msq[:, :], mrow[:, :], mrow[:, :],
                                                           ALU.mult))
                    t_v = pl.op("dve", lambda e: e.scalar_tensor_tensor(
                        varr[:, :], psO[:, TT:2 * TT], 1.0 / D, msq[:, :],
                        ALU.mult, ALU.subtract), inc=True)
                    t_stats_free = t_v
                    pl.op("act", lambda e: e.activation(rstd[:, :], varr[:, :], AF.Ln,
                                                        bias=lnmisc[:, 0:1], scale=lnmisc[:, 2:3]), waits=tmax(t_v))
                    t_ln = pl.op("act", lambda e: e.activation(rstd[:, :], rstd[:, :], AF.Exp,
                                                               bias=0.0, scale=lnmisc[:, 1:2]), inc=True)
                    pl.op("dve", lambda e: e.tensor_tensor(xb[:, :, :], xb[:, :, :], m_bc,
                                                           ALU.subtract), waits=tmax(t_m))
                    t_xn = pl.op("dve", lambda e: e.tensor_tensor(xb[:, :, :], xb[:, :, :],
                                                                  r_bc, ALU.mult),
                                 waits=tmax(t_ln), inc=True)
                    if dump is not None and dump[0] == "ffn_xn" and tt_i == 0:
                        pl.barrier()
                        pl.dma("dsp", dbg.rearrange("(i p) t -> p i t", p=128), xb[:, :, :], chan="dbg")
                        pl.barrier()
                        stop_flag[0] = True
                        return
                    if DEBUG_SERIAL[0]:
                        pl.barrier()

                    for gp in range(16):
                        ri = ring[0] % R3
                        ring[0] += 1
                        t_wl = pl.dma("dsp", wr[ri][:, :, :],
                                      k0fv[:, :, gp * 256:(gp + 1) * 256],
                                      waits=tmax(wr_free[ri], t_k0_done[0]),
                                      chan="wr" + str(ri))
                        ps_i = gp % R3
                        for ic in range(8):
                            for oc in range(2):
                                def t_m0(e, ri=ri, ic=ic, oc=oc, ps_i=ps_i):
                                    return e.matmul(psS[ps_i][:, oc * TT:(oc + 1) * TT],
                                                    wr[ri][:, ic, oc * 128:(oc + 1) * 128],
                                                    xb[:, ic, :],
                                                    start=(ic == 0), stop=(ic == 7))
                                w = tmax(t_wl, t_xn, ps_free[ps_i]) \
                                    if (ic == 0 and oc == 0) else ()
                                tk = pl.op("pe", t_m0, waits=w,
                                           inc=(ic == 7 and oc == 1))
                        wr_free[ri] = tk
                        def t_e0(e, gp=gp, ps_i=ps_i):
                            return e.activation(h1[:, 2 * gp:2 * gp + 2, :], psv[ps_i],
                                                AF.Relu, bias=0.0, scale=s0x4[:, :])
                        ps_free[ps_i] = pl.op("act", t_e0, waits=tmax(tk, t_sv), inc=True)
                        if DEBUG_SERIAL[0] >= 2:
                            pl.barrier()
                    t_prev_mm0 = tk
                    t_h1 = [ps_free[0], ps_free[1], ps_free[2]]
                    if dump is not None and dump[0] == "ffn_h1" and tt_i == 0:
                        pl.barrier()
                        pl.dma("dsp", dbg.rearrange("(i p) t -> p i t", p=128), h1[:, :, :], chan="dbg")
                        pl.barrier()
                        stop_flag[0] = True
                        return
                    if DEBUG_SERIAL[0]:
                        pl.barrier()

                    for gp in range(16):
                        ps_i = gp % R3
                        for icq in range(4):
                            ri = ring[0] % R3
                            ring[0] += 1
                            t_wl = pl.dma("dsp", wr[ri][:, :, :],
                                          w1v[:, icq * 8:(icq + 1) * 8,
                                              gp * 256:(gp + 1) * 256],
                                          waits=tmax(wr_free[ri], t_w1_done[0]),
                                          chan="wr" + str(ri))
                            for ic8 in range(8):
                                ic = icq * 8 + ic8
                                for oc in range(2):
                                    def t_m1(e, ri=ri, ic8=ic8, ic=ic, oc=oc, ps_i=ps_i):
                                        return e.matmul(
                                            psS[ps_i][:, oc * TT:(oc + 1) * TT],
                                            wr[ri][:, ic8, oc * 128:(oc + 1) * 128],
                                            h1[:, ic, :],
                                            start=(ic == 0), stop=(ic == 31))
                                    w = ()
                                    if ic8 == 0 and oc == 0:
                                        w = tmax(t_wl,
                                                 tmax(*t_h1) if gp == 0 and icq == 0 else None,
                                                 ps_free[ps_i] if icq == 0 else None)
                                    tk = pl.op("pe", t_m1, waits=w,
                                               inc=(ic8 == 7 and oc == 1))
                            wr_free[ri] = tk
                        def t_e1(e, gp=gp, ps_i=ps_i):
                            return e.activation(h2[:, 2 * gp:2 * gp + 2, :], psv[ps_i],
                                                AF.Relu, bias=0.0, scale=s1x2[:, :])
                        ps_free[ps_i] = pl.op("act", t_e1, waits=tmax(tk), inc=True)
                        if DEBUG_SERIAL[0] >= 2:
                            pl.barrier()
                    t_h2 = [ps_free[0], ps_free[1], ps_free[2]]
                    if dump is not None and dump[0] == "ffn_h2" and tt_i == 0:
                        pl.barrier()
                        pl.dma("dsp", dbg.rearrange("(i p) t -> p i t", p=128), h2[:, :, :], chan="dbg")
                        pl.barrier()
                        stop_flag[0] = True
                        return
                    if DEBUG_SERIAL[0]:
                        pl.barrier()

                    for gp in range(4):
                        ps_i = gp % R3
                        for icq in range(4):
                            ri = ring[0] % R3
                            ring[0] += 1
                            t_wl = pl.dma("dsp", wr[ri][:, :, :],
                                          w2v[:, icq * 8:(icq + 1) * 8,
                                              gp * 256:(gp + 1) * 256],
                                          waits=tmax(wr_free[ri], t_w2_done[0]),
                                          chan="wr" + str(ri))
                            for ic8 in range(8):
                                ic = icq * 8 + ic8
                                for oc in range(2):
                                    def t_m2(e, ri=ri, ic8=ic8, ic=ic, oc=oc, ps_i=ps_i):
                                        return e.matmul(
                                            psS[ps_i][:, oc * TT:(oc + 1) * TT],
                                            wr[ri][:, ic8, oc * 128:(oc + 1) * 128],
                                            h2[:, ic, :],
                                            start=(ic == 0), stop=(ic == 31))
                                    w = ()
                                    if ic8 == 0 and oc == 0:
                                        w = tmax(t_wl,
                                                 tmax(*t_h2) if gp == 0 and icq == 0 else None,
                                                 ps_free[ps_i] if icq == 0 else None)
                                    tk = pl.op("pe", t_m2, waits=w,
                                               inc=(ic8 == 7 and oc == 1))
                            wr_free[ri] = tk
                        def t_e2(e, gp=gp, ps_i=ps_i):
                            return e.scalar_tensor_tensor(
                                xt[:, 2 * gp:2 * gp + 2, :], psv[ps_i], s2h[:, :],
                                xt[:, 2 * gp:2 * gp + 2, :], ALU.mult, ALU.add)
                        ps_free[ps_i] = pl.op("dve", t_e2, waits=tmax(tk), inc=True)
                        if DEBUG_SERIAL[0] >= 2:
                            pl.barrier()
                    t_prev_store = pl.dma("dsp", xfm[:, :, tsl], xt[:, :, :],
                                          waits=tmax(ps_free[0], ps_free[1], ps_free[2]),
                                          chan="xs")
                    if DEBUG_SERIAL[0]:
                        pl.barrier()
                pl.barrier()

        stop_flag = [False]
        emit_ffn()
        if stop_flag[0]:
            pl.emit(block)
            return nc, pl.nops
        if dump is not None and dump[0] == "x1":
            r = _dump_and_stop(x_dram.ap())
            if r:
                return r

        def emit_wload(pl_, wstg, dram_ap, dest_sb, n_ic, C, dest_off=0):
            stg_tok = [None, None]
            tk = None
            for ic in range(n_ic):
                bi = ic % 2
                t_l = pl_.dma("dsp", wstg[bi][:, 0:C],
                              dram_ap[ic * 128:(ic + 1) * 128, :],
                              waits=tmax(stg_tok[bi]), chan="cst")
                def t_cv(e, bi=bi, ic=ic, C=C, wstg=wstg, dest_sb=dest_sb,
                         dest_off=dest_off):
                    return e.tensor_copy(dest_sb[:, ic, dest_off:dest_off + C],
                                         wstg[bi][:, 0:C])
                tk = pl_.op("dve", t_cv, waits=tmax(t_l), inc=True)
                stg_tok[bi] = tk
            return tk

        with ExitStack() as st2:
            E2 = st2.enter_context
            xn2 = E2(nc.sbuf_tensor("xn2", [128, 8, TPC], BF16))
            lat = E2(nc.sbuf_tensor("lat", [128, 2, TT], BF16))
            xt = E2(nc.sbuf_tensor("xt2", [128, 8, TT], F32))
            xsq = E2(nc.sbuf_tensor("xsq2", [128, 8, TT], BF16))
            mrow = E2(nc.sbuf_tensor("mrow2", [128, TT], F32))
            msq = E2(nc.sbuf_tensor("msq2", [128, TT], F32))
            varr = E2(nc.sbuf_tensor("varr2", [128, TT], F32))
            rstd = E2(nc.sbuf_tensor("rstd2", [128, TT], F32))
            kvt = E2(nc.sbuf_tensor("kvt", [128, 8, TT], BF16))
            wk_sb = E2(nc.sbuf_tensor("wk_sb", [128, 8, D], BF16))
            wv_sb = E2(nc.sbuf_tensor("wv_sb", [128, 8, D], BF16))
            wd_sb = E2(nc.sbuf_tensor("wd_sb", [128, 8, DL], BF16))
            m_bc = mrow[:, :].unsqueeze(1).broadcast_to([128, 8, TT])
            r_bc = rstd[:, :].unsqueeze(1).broadcast_to([128, 8, TT])

            wstg2 = [E2(nc.sbuf_tensor(f"wstg2_{i}", [128, D], F32)) for i in range(2)]
            t_wk = emit_wload(pl, wstg2, wk, wk_sb, 8, D)
            t_wv = emit_wload(pl, wstg2, wv, wv_sb, 8, D)
            t_wd = emit_wload(pl, wstg2, wd, wd_sb, 8, DL)
            ps_free = [None] * R3
            t_stats_free = None
            t_kvt_free = None
            t_lat_free = None
            for tt_i in range(NTT):
                tsl = slice(tt_i * TT, (tt_i + 1) * TT)
                xbv = xn2[:, :, tsl]
                t_x = pl.dma("dsp", xt[:, :, :], xfm[:, :, tsl], chan="xl")
                t_xb = pl.op("dve", lambda e, xbv=xbv: e.tensor_copy(xbv, xt[:, :, :]),
                             waits=tmax(t_x), inc=True)
                t_sq = pl.op("act", lambda e, xbv=xbv: e.activation(xsq[:, :, :], xbv,
                                                                    AF.Square),
                             waits=tmax(t_xb), inc=True)
                for ic in range(8):
                    def t_s(e, ic=ic, xbv=xbv):
                        return e.matmul(psO[:, 0:TT], ones_bf[:, :], xbv[:, ic, :],
                                        start=(ic == 0), stop=(ic == 7))
                    pl.op("pe", t_s, waits=tmax(t_xb, t_stats_free) if ic == 0 else ())
                for ic in range(8):
                    def t_s(e, ic=ic):
                        return e.matmul(psO[:, TT:2 * TT], ones_bf[:, :], xsq[:, ic, :],
                                        start=(ic == 0), stop=(ic == 7))
                    tk = pl.op("pe", t_s, waits=tmax(t_sq) if ic == 0 else (), inc=(ic == 7))
                t_m = pl.op("dve", lambda e: e.tensor_scalar(mrow[:, :], psO[:, 0:TT],
                                                             1.0 / D, None, ALU.mult),
                            waits=tmax(tk), inc=True)
                pl.op("dve", lambda e: e.tensor_tensor(msq[:, :], mrow[:, :], mrow[:, :],
                                                       ALU.mult))
                t_v = pl.op("dve", lambda e: e.scalar_tensor_tensor(
                    varr[:, :], psO[:, TT:2 * TT], 1.0 / D, msq[:, :],
                    ALU.mult, ALU.subtract), inc=True)
                t_stats_free = t_v
                pl.op("act", lambda e: e.activation(rstd[:, :], varr[:, :], AF.Ln,
                                                    bias=lnmisc[:, 0:1], scale=lnmisc[:, 2:3]), waits=tmax(t_v))
                t_ln = pl.op("act", lambda e: e.activation(rstd[:, :], rstd[:, :], AF.Exp,
                                                           bias=0.0, scale=lnmisc[:, 1:2]), inc=True)
                pl.op("dve", lambda e, xbv=xbv: e.tensor_tensor(xbv, xbv, m_bc, ALU.subtract),
                      waits=tmax(t_m))
                t_xn = pl.op("dve", lambda e, xbv=xbv: e.tensor_tensor(xbv, xbv, r_bc,
                                                                       ALU.mult),
                             waits=tmax(t_ln), inc=True)
                pl.dma("dsp", ag1inx[:, :, tsl], xbv, waits=tmax(t_xn), chan="ag")
                for gp in range(4):
                    ps_i = gp % R3
                    for wsel in range(2):
                        wsb = wk_sb if wsel == 0 else wv_sb
                        for ic in range(8):
                            for oc in range(2):
                                occ = gp * 2 + oc
                                def t_kv(e, wsb=wsb, ic=ic, occ=occ, oc=oc, ps_i=ps_i,
                                         wsel=wsel, xbv=xbv):
                                    return e.matmul(psS[ps_i][:, oc * TT:(oc + 1) * TT],
                                                    wsb[:, ic, occ * 128:(occ + 1) * 128],
                                                    xbv[:, ic, :],
                                                    start=(wsel == 0 and ic == 0),
                                                    stop=(wsel == 1 and ic == 7))
                                w = ()
                                if wsel == 0 and ic == 0 and oc == 0:
                                    w = tmax(t_xn, t_wk, t_wv, ps_free[ps_i], t_kvt_free)
                                tk = pl.op("pe", t_kv, waits=w,
                                           inc=(wsel == 1 and ic == 7 and oc == 1))
                    def t_ekv(e, gp=gp, ps_i=ps_i):
                        return e.tensor_copy(kvt[:, 2 * gp:2 * gp + 2, :], psv[ps_i])
                    ps_free[ps_i] = pl.op("dve", t_ekv, waits=tmax(tk), inc=True)
                t_kvt = [ps_free[0], ps_free[1], ps_free[2]]
                ps_i = 0
                for ic in range(8):
                    for oc in range(2):
                        def t_lt(e, ic=ic, oc=oc, ps_i=ps_i):
                            return e.matmul(psS[ps_i][:, oc * TT:(oc + 1) * TT],
                                            wd_sb[:, ic, oc * 128:(oc + 1) * 128],
                                            kvt[:, ic, :],
                                            start=(ic == 0), stop=(ic == 7))
                        w = tmax(tmax(*t_kvt), t_wd, t_lat_free) \
                            if (ic == 0 and oc == 0) else ()
                        tk = pl.op("pe", t_lt, waits=w, inc=(ic == 7 and oc == 1))
                t_kvt_free = tk
                t_latv = pl.op("dve", (lambda e, ps_i=ps_i:
                                       e.tensor_copy(lat[:, :, :], psv[ps_i])),
                               waits=tmax(tk, t_lat_free), inc=True)
                ps_free[0] = t_latv
                t_lat_free = pl.dma("dsp", ag1inl[:, :, tsl], lat[:, :, :],
                                    waits=tmax(t_latv), chan="ag")
            pl.barrier()

        if dump is not None and dump[0] == "ag1in":
            r = _dump_and_stop(ag1in.ap())
            if r:
                return r

        pl.collective("AllGather", ALU.bypass, groups, [ag1in.ap().opt()],
                      [ag1out.ap().opt()])
        pl.barrier()

        with ExitStack() as st4:
            E4 = st4.enter_context
            qup = E4(nc.sbuf_tensor("qup", [128, 2, S], BF16))
            qdn = E4(nc.sbuf_tensor("qdn", [128, 2, S], BF16))
            lup = E4(nc.sbuf_tensor("lup", [128, 2, S], BF16))
            khat = E4(nc.sbuf_tensor("khat", [128, 2, S], BF16))
            vpair = E4(nc.sbuf_tensor("vpair", [128, NKC * 256], BF16))
            pbuf = [E4(nc.sbuf_tensor(f"pbuf{i}", [128, 1024], BF16)) for i in range(R3)]
            rcpb = E4(nc.sbuf_tensor("rcpb", [128, 1024], F32))
            astage = E4(nc.sbuf_tensor("astage", [128, 1024], BF16))
            wq_sb = E4(nc.sbuf_tensor("wq_sb", [128, 8, 256], BF16))
            wu_sb = E4(nc.sbuf_tensor("wu_sb", [128, 2, 512], BF16))
            rq = [E4(nc.sbuf_tensor(f"rq{i}", [128, TT], BF16)) for i in range(R3)]
            vpv = vpair[:, :].rearrange("p (n c) -> p n c", c=256)

            wstg4 = [E4(nc.sbuf_tensor(f"wstg4_{i}", [128, 256], F32)) for i in range(2)]
            t_wq = emit_wload(pl, wstg4, wq, wq_sb, 8, 256)
            t_wuk = emit_wload(pl, wstg4, wuk, wu_sb, 2, 256, dest_off=0)
            t_wuv = emit_wload(pl, wstg4, wuv, wu_sb, 2, 256, dest_off=256)

            t_lat = [pl.dma("dsp", lup[:, :, r * TPC:(r + 1) * TPC], ag1v[r, :, 8:10, :],
                             chan="lat") for r in range(4)]
            lsc_bc = latsc[:, :].unsqueeze(1).unsqueeze(1) \
                .broadcast_to([128, 2, S // 128, 128])
            lupv = lup[:, :, :].rearrange("p i (n c) -> p i n c", c=128)
            t_lup = pl.op("dve", lambda e: e.tensor_tensor(lupv, lupv, lsc_bc, ALU.mult),
                          waits=tmax(tmax(*t_lat), tmax(t_con)), inc=True)

            ps_free = [None] * R3
            rq_free = [None] * R3
            gi = 0
            for r in range(4):
                for tch in range(4):
                    tsl_g = slice(r * TPC + tch * TT, r * TPC + (tch + 1) * TT)
                    tsl_l = slice(tch * TT, (tch + 1) * TT)
                    ps_i = gi % R3
                    for ic in range(8):
                        rri = (gi * 8 + ic) % R3
                        t_rl = pl.dma("dsp", rq[rri][:, :], ag1v[r, :, ic, tsl_l],
                                      waits=tmax(rq_free[rri]), chan="q" + str(rri))
                        for oc in range(2):
                            def t_q(e, rri=rri, ic=ic, oc=oc, ps_i=ps_i):
                                return e.matmul(psS[ps_i][:, oc * TT:(oc + 1) * TT],
                                                wq_sb[:, ic, oc * 128:(oc + 1) * 128],
                                                rq[rri][:, :],
                                                start=(ic == 0), stop=(ic == 7))
                            w = tmax(t_rl, t_wq if gi == 0 and ic == 0 else None,
                                     ps_free[ps_i] if ic == 0 and oc == 0 else None)
                            tk = pl.op("pe", t_q, waits=w, inc=(oc == 1))
                        rq_free[rri] = tk
                    def t_eq(e, tsl_g=tsl_g, ps_i=ps_i):
                        return e.tensor_copy(qdn[:, :, tsl_g], psv[ps_i])
                    ps_free[ps_i] = pl.op("dve", t_eq, waits=tmax(tk), inc=True)
                    gi += 1
            t_qraw = [ps_free[0], ps_free[1], ps_free[2]]
            qup_bc = qsc[:, QT:2 * QT].unsqueeze(1).unsqueeze(1) \
                .broadcast_to([128, 2, S // QT, QT])
            qdn_bc = qsc[:, 0:QT].unsqueeze(1).unsqueeze(1) \
                .broadcast_to([128, 2, S // QT, QT])
            qupv = qup[:, :, :].rearrange("p i (n c) -> p i n c", c=QT)
            qdnv = qdn[:, :, :].rearrange("p i (n c) -> p i n c", c=QT)
            t_qup = pl.op("dve", lambda e: e.tensor_tensor(qupv, qdnv, qup_bc, ALU.mult),
                          waits=tmax(t_qraw, tmax(t_con)), inc=True)
            t_qdn = pl.op("dve", lambda e: e.tensor_tensor(qdnv, qdnv, qdn_bc, ALU.mult),
                          inc=True)

            ps_free2 = [None] * R3
            for tch in range(16):
                tsl_g = slice(tch * TT, (tch + 1) * TT)
                ps_i = tch % R3
                for ic in range(2):
                    for oc in range(2):
                        def t_k(e, ic=ic, oc=oc, ps_i=ps_i, tsl_g=tsl_g):
                            return e.matmul(psS[ps_i][:, oc * TT:(oc + 1) * TT],
                                            wu_sb[:, ic, oc * 128:(oc + 1) * 128],
                                            lup[:, ic, tsl_g],
                                            start=(ic == 0), stop=(ic == 1))
                        w = ()
                        if ic == 0 and oc == 0:
                            w = tmax(t_lup, t_wuk, ps_free2[ps_i], ps_free[ps_i], t_qdn)
                        tk = pl.op("pe", t_k, waits=w, inc=(ic == 1 and oc == 1))
                def t_ek(e, tsl_g=tsl_g, ps_i=ps_i):
                    return e.tensor_copy(khat[:, :, tsl_g], psv[ps_i])
                ps_free2[ps_i] = pl.op("dve", t_ek, waits=tmax(tk), inc=True)
            t_khat = [ps_free2[0], ps_free2[1], ps_free2[2]]
            pl.barrier()
            if dump is not None and dump[0] in ("qup", "qdn", "khat", "lup"):
                src = {"qup": qup, "qdn": qdn, "khat": khat, "lup": lup}[dump[0]]
                pl.dma("dsp", dbg.rearrange("(i p) t -> p i t", p=128), src[:, :, :], chan="dbg")
                pl.barrier()
                pl.emit(block)
                return nc, pl.nops

            vpv4 = vpair[:, :].rearrange("p (n g c) -> p n g c", c=64, g=4)
            t_vones = pl.op("dve", lambda e: e.memset(vpv4[:, :, 1:4:2, :], 1.0), inc=True)

            t_pair_fence = None
            n_pairs, n_qts = 2, NQT
            if dump is not None and dump[0].startswith("atmy_p"):
                n_pairs = int(dump[0][6])
                n_qts = int(dump[0][8:])
            for pair in range(n_pairs):
                vg_tok = None
                for kc in range(NKC):
                    ksl = slice(kc * 128, (kc + 1) * 128)
                    for ic in range(2):
                        def t_vg(e, ic=ic, ksl=ksl, pair=pair):
                            return e.matmul(psO[:, 0:128], lup[:, ic, ksl],
                                            wu_sb[:, ic, 256 + pair * 128:
                                                  256 + (pair + 1) * 128],
                                            start=(ic == 0), stop=(ic == 1))
                        w = ()
                        if ic == 0:
                            w = tmax(vg_tok,
                                     tmax(t_lup, t_wuv, t_pair_fence, t_vones)
                                     if kc == 0 else None)
                        tk = pl.op("pe", t_vg, waits=w, inc=(ic == 1))
                    def t_ev(e, kc=kc):
                        dst = vpair[:, kc * 256:kc * 256 + 256].rearrange(
                            "p (g c) -> p g c", c=64)[:, 0:3:2, :]
                        src = psO[:, 0:128].rearrange("p (g c) -> p g c", c=64)
                        return e.tensor_scalar(dst, src, vundo[:, :], None, ALU.mult)
                    vg_tok = pl.op("dve", t_ev, waits=tmax(tk), inc=True)
                t_vgen = vg_tok

                sfree = [None] * R3
                pfree = [None] * R3
                t_psO_free = t_vgen
                t_ast_store = None
                it = 0
                for qt in range(n_qts):
                    qsl = slice(qt * QT, (qt + 1) * QT)
                    last_av = None
                    for kc in range(NKC):
                        i3 = it % R3
                        ksl = slice(kc * 128, (kc + 1) * 128)
                        delta = qt * 4 - kc
                        if delta >= 1:
                            case, qsrc = "L", qdn
                        elif delta <= -4:
                            case, qsrc = "U", qup
                        else:
                            case, qsrc = "M", qdn
                        w0 = tmax(sfree[i3],
                                  (tmax(t_vgen, t_qup, t_qdn, t_khat)
                                   if it == 0 else None))
                        for half in range(2):
                            hs0, hs1 = half * 64, half * 64 + 64
                            def t_sc(e, i3=i3, hs0=hs0, hs1=hs1, ksl=ksl, qsl=qsl,
                                     half=half, pair=pair, qsrc=qsrc):
                                return e.matmul(psS[i3][:, half * QT:(half + 1) * QT],
                                                khat[hs0:hs1, pair, ksl],
                                                qsrc[hs0:hs1, pair, qsl],
                                                start=True, stop=True,
                                                tile_position=(hs0, 0))
                            tk = pl.op("pe", t_sc, waits=(w0 if half == 0 else ()),
                                       inc=(half == 1))
                        t_mm = tk
                        t_pre = t_mm
                        if case == "M":
                            dpr = kc - qt * 4
                            c_bc = corr[:, dpr * QT:(dpr + 1) * QT].unsqueeze(1) \
                                .broadcast_to([128, 2, QT])
                            def t_co(e, i3=i3, c_bc=c_bc):
                                return e.tensor_tensor(psv[i3], psv[i3], c_bc, ALU.mult)
                            t_pre = pl.op("dve", t_co, waits=tmax(t_mm), inc=True)
                        pw = tmax(t_pre, pfree[i3])
                        if case == "U":
                            m = -delta
                            def t_ex(e, i3=i3, m=m):
                                return e.activation(pbuf[i3][:, :], psS[i3][:, :], AF.Exp,
                                                    bias=0.0, scale=svec[:, m:m + 1])
                        elif case == "L":
                            def t_ex(e, i3=i3, m=delta):
                                return e.activation(pbuf[i3][:, :], psS[i3][:, :], AF.Exp,
                                                    bias=0.0, scale=ldsc[:, m:m + 1])
                        else:
                            def t_ex(e, i3=i3):
                                return e.activation(pbuf[i3][:, :], psS[i3][:, :], AF.Exp,
                                                    bias=0.0, scale=lnmisc[:, 2:3])
                        t_exp = pl.op("act", t_ex, waits=pw, inc=True)
                        sfree[i3] = t_exp
                        for half in range(2):
                            base = kc * 256 + half * 128
                            lv = vpair[:, base:base + 128]
                            def t_av(e, lv=lv, i3=i3, half=half, kc=kc):
                                return e.matmul(psO[:, half * QT:(half + 1) * QT], lv,
                                                pbuf[i3][:, half * QT:(half + 1) * QT],
                                                start=(kc == 0), stop=(kc == NKC - 1))
                            w = tmax(t_exp,
                                     t_psO_free if (kc == 0 and half == 0) else None)
                            tk = pl.op("pe", t_av, waits=w, inc=(half == 1))
                        pfree[i3] = tk
                        last_av = tk
                        it += 1
                    ni = it % R3
                    t_rc = pl.op("dve",
                                 lambda e: e.reciprocal(rcpb[64:128, :], psO[64:128, :]),
                                 waits=tmax(last_av), inc=True)
                    for half in range(2):
                        def t_nb(e, half=half, ni=ni):
                            return e.matmul(psS[ni][0:64, half * QT:(half + 1) * QT],
                                            m64[64:128, :],
                                            rcpb[64:128, half * QT:(half + 1) * QT],
                                            start=True, stop=True,
                                            tile_position=(64, 0))
                        tk = pl.op("pe", t_nb,
                                   waits=tmax(t_rc, sfree[ni]) if half == 0 else (),
                                   inc=(half == 1))
                    t_rcb = pl.op("dve",
                                  (lambda e, ni=ni:
                                   e.tensor_copy(rcpb[0:64, :], psS[ni][0:64, :])),
                                  waits=tmax(tk), inc=True)
                    sfree[ni] = t_rcb
                    t_norm = pl.op("dve",
                                   lambda e: e.tensor_tensor(astage[0:64, :],
                                                             psO[0:64, :],
                                                             rcpb[0:64, :], ALU.mult),
                                   waits=tmax(t_rcb, t_ast_store), inc=True)
                    t_psO_free = t_norm
                    st_t = []
                    for half in range(2):
                        hrow = (2 * pair + half) * 64
                        st_t.append(pl.dma(
                            "dsp", atmy_dram.ap()[hrow:hrow + 64, qsl],
                            astage[0:64, half * QT:(half + 1) * QT],
                            waits=tmax(t_norm), chan="as"))
                    t_ast_store = tmax(*st_t)
                t_pair_fence = [t_norm, last_av]
            pl.barrier()

        if dump is not None and (dump[0] == "atmy" or dump[0].startswith("atmy_p")):
            r = _dump_and_stop(atmy_dram.ap())
            if r:
                return r

        with ExitStack() as st6:
            E6 = st6.enter_context
            at_sb = E6(nc.sbuf_tensor("at_sb", [128, 2, S], BF16))
            wo_sb = E6(nc.sbuf_tensor("wo_sb", [128, 2, D], BF16))
            rstage = [E6(nc.sbuf_tensor(f"rstage{i}", [128, 1024], BF16)) for i in range(2)]
            wstg6 = [E6(nc.sbuf_tensor(f"wstg6_{i}", [128, D], F32)) for i in range(2)]
            t_wo = emit_wload(pl, wstg6, wo, wo_sb, 2, D)
            t_at = pl.dma("dsp", at_sb[:, :, :], atmyv, chan="at")
            ps_free = [None] * R3
            rs_free = [None, None]
            for tch in range(S // 128):
                ps_i = tch % R3
                bi = tch % 2
                for half in range(2):
                    for ic in range(2):
                        def t_wp(e, ic=ic, half=half, ps_i=ps_i, tch=tch):
                            return e.matmul(psS[ps_i][:, half * TT:(half + 1) * TT],
                                            at_sb[:, ic, tch * 128:(tch + 1) * 128],
                                            wo_sb[:, ic, half * TT:(half + 1) * TT],
                                            start=(ic == 0), stop=(ic == 1))
                        w = tmax(t_at, t_wo, ps_free[ps_i]) \
                            if (half == 0 and ic == 0) else ()
                        tk = pl.op("pe", t_wp, waits=w, inc=(half == 1 and ic == 1))
                def t_er(e, bi=bi, ps_i=ps_i):
                    return e.tensor_copy(rstage[bi][:, :], psS[ps_i][:, :])
                t_e = pl.op("dve", t_er, waits=tmax(tk, rs_free[bi]), inc=True)
                ps_free[ps_i] = t_e
                rs_free[bi] = pl.dma("dsp", rsin.ap()[tch * 128:(tch + 1) * 128, :],
                                     rstage[bi][:, :], waits=tmax(t_e), chan="rs")
            pl.barrier()
            pl.collective("ReduceScatter", ALU.add, groups, [rsin.ap().opt()],
                          [rsout.ap().opt()])
            pl.barrier()

        if dump is not None and dump[0] == "rsout":
            r = _dump_and_stop(rsout.ap())
            if r:
                return r

        with ExitStack() as st7:
            E7 = st7.enter_context
            rt = [E7(nc.sbuf_tensor(f"rt{i}", [128, 1024], F32)) for i in range(2)]
            rtb = [E7(nc.sbuf_tensor(f"rtb{i}", [128, 1024], BF16)) for i in range(2)]
            xtile = [E7(nc.sbuf_tensor(f"xtile{i}", [128, 8, 128], F32)) for i in range(2)]
            rt_free = [None, None]
            rtb_free = [None, None]
            xt_free = [None, None]
            tens4 = psS + [psO]
            t_evprev = None
            for tch in range(TPC // 128):
                bi = tch % 2
                t_rb = pl.dma("dsp", rtb[bi][:, :], rsout.ap()[tch * 128:(tch + 1) * 128, :],
                              waits=tmax(rtb_free[bi]), chan="rt")
                def t_rcv(e, bi=bi):
                    return e.tensor_copy(rt[bi][:, :], rtb[bi][:, :])
                t_r = pl.op("dve", t_rcv, waits=tmax(t_rb, rt_free[bi]), inc=True)
                rtb_free[bi] = t_r
                t_xl = pl.dma("dsp", xtile[bi][:, :, :],
                              xfm[:, :, tch * 128:(tch + 1) * 128],
                              waits=tmax(xt_free[bi]), chan="xl")
                for ic in range(8):
                    tns = tens4[ic // 2]
                    co = (ic % 2) * 512
                    def t_tr(e, ic=ic, bi=bi, tns=tns, co=co):
                        return e.transpose(tns[:, co:co + 128],
                                           rt[bi][:, ic * 128:(ic + 1) * 128], id_f[:, :])
                    w = tmax(t_r, t_evprev) if ic == 0 else ()
                    tk = pl.op("pe", t_tr, waits=w, inc=(ic == 7))
                rt_free[bi] = tk
                for j in range(4):
                    def t_add(e, bi=bi, j=j, tns=tens4[j]):
                        pv = tns[:, :].rearrange("p (s c) -> p s c", s=2)[:, :, 0:128]
                        xv = xtile[bi][:, 2 * j:2 * j + 2, :]
                        return e.tensor_tensor(xv, xv, pv, ALU.add)
                    t_a = pl.op("dve", t_add,
                                waits=tmax(tk, t_xl) if j == 0 else (), inc=(j == 3))
                t_evprev = t_a
                xt_free[bi] = pl.dma("dsp", xfm[:, :, tch * 128:(tch + 1) * 128],
                                     xtile[bi][:, :, :], waits=tmax(t_a), chan="xs")
            pl.barrier()

        if dump is not None and dump[0] == "x2":
            r = _dump_and_stop(x_dram.ap())
            if r:
                return r

        emit_ffn()
        if dump is not None and dump[0] == "x3":
            r = _dump_and_stop(x_dram.ap())
            if r:
                return r

        with ExitStack() as st9:
            E9 = st9.enter_context
            t9in = [E9(nc.sbuf_tensor(f"oin{i}", [128, 8, 128], F32)) for i in range(2)]
            t9out = [E9(nc.sbuf_tensor(f"oout{i}", [128, 1024], F32)) for i in range(2)]
            t9in_tok = [None, None]
            t9out_tok = [None, None]
            tens4 = psS + [psO]
            t_evprev = None
            for tch in range(TPC // 128):
                bi = tch % 2
                t_ld = pl.dma("dsp", t9in[bi][:, :, :], xfm[:, :, tch * 128:(tch + 1) * 128],
                              waits=tmax(t9in_tok[bi]), chan="xl")
                for ic in range(8):
                    tns = tens4[ic // 2]
                    co = (ic % 2) * 512
                    def t_tr(e, ic=ic, bi=bi, tns=tns, co=co):
                        return e.transpose(tns[:, co:co + 128],
                                           t9in[bi][:, ic, :], id_f[:, :])
                    w = tmax(t_ld, t_evprev) if ic == 0 else ()
                    tk = pl.op("pe", t_tr, waits=w, inc=(ic == 7))
                t9in_tok[bi] = tk
                for j in range(4):
                    def t_ev(e, bi=bi, j=j, tns=tens4[j]):
                        src = tns[:, :].rearrange("p (s c) -> p s c", s=2)[:, :, 0:128]
                        dst = t9out[bi][:, j * 256:(j + 1) * 256].rearrange(
                            "p (s c) -> p s c", c=128)
                        return e.tensor_copy(dst, src)
                    t_e = pl.op("dve", t_ev,
                                waits=tmax(tk, t9out_tok[bi]) if j == 0 else (),
                                inc=(j == 3))
                t_evprev = t_e
                t9out_tok[bi] = pl.dma("dsp", y[tch * 128:(tch + 1) * 128, :],
                                      t9out[bi][:, :], waits=tmax(t_e), chan="out")
            pl.barrier()

        pl.emit(block)
    return nc, pl.nops


def _consts():
    lam = LAM
    u = np.arange(QT, dtype=np.float64)
    qsc_dn = np.exp(-lam * u) / 32.0
    qsc_up = np.exp(lam * u) / 32.0
    qsc = np.broadcast_to(np.concatenate([qsc_dn, qsc_up])[None, :], (128, 2 * QT))
    latsc = np.broadcast_to(np.exp(lam * np.arange(128, dtype=np.float64))[None, :],
                            (128, 128))
    vundo = np.exp(-lam * np.arange(128, dtype=np.float64))[:, None]
    vv = np.arange(128, dtype=np.float64)[:, None]
    uu = np.arange(QT, dtype=np.float64)[None, :]
    tiles = []
    for dp in (0, 128, 256, 384):
        w = uu - vv
        c = np.where(w >= dp, np.exp(lam * dp), np.exp(2 * lam * w - lam * dp))
        tiles.append(c)
    corr = np.concatenate(tiles, axis=1)
    mm_ = np.arange(64, dtype=np.float64)[None, :]
    svec = np.exp(-lam * 128.0 * mm_ - 2.0 * lam * vv)
    ldsc = np.broadcast_to(np.exp(-lam * 128.0 * np.arange(64, dtype=np.float64))[None, :],
                           (128, 64))
    return (np.ascontiguousarray(qsc, dtype=np.float32),
            np.ascontiguousarray(latsc, dtype=np.float32),
            np.ascontiguousarray(vundo, dtype=np.float32),
            np.ascontiguousarray(corr, dtype=np.float32),
            np.ascontiguousarray(svec, dtype=np.float32),
            np.ascontiguousarray(ldsc, dtype=np.float32))


def make_in_maps(inputs):
    x = np.asarray(inputs["L0_attn_x"], dtype=np.float32)
    qsc, latsc, vundo, corr, svec, ldsc = _consts()
    ident = np.eye(128, dtype=np.float32)
    com = {
        "wk": np.ascontiguousarray(inputs["L0_attn_W_k"], dtype=np.float32),
        "wv": np.ascontiguousarray(inputs["L0_attn_W_v"], dtype=np.float32),
        "wd": np.ascontiguousarray(inputs["L0_attn_W_down_kv"], dtype=np.float32),
        "k0u": np.ascontiguousarray(np.asarray(inputs["L0_ffn_k0"]).view(np.uint8)),
        "k1u": np.ascontiguousarray(np.asarray(inputs["L0_ffn_k1"]).view(np.uint8)),
        "k2u": np.ascontiguousarray(np.asarray(inputs["L0_ffn_k2"]).view(np.uint8)),
        "s0in": np.ascontiguousarray(
            np.broadcast_to(np.asarray(inputs["L0_ffn_s0"], dtype=np.float32), (128, 1))),
        "s1in": np.ascontiguousarray(
            np.broadcast_to(np.asarray(inputs["L0_ffn_s1"], dtype=np.float32), (128, 1))),
        "s2in": np.ascontiguousarray(
            np.broadcast_to(np.asarray(inputs["L0_ffn_s2"], dtype=np.float32), (128, 1))),
        "ident": ident, "qsc_i": qsc, "latsc_i": latsc, "vundo_i": vundo,
        "corr_i": corr, "svec_i": svec, "ldsc_i": ldsc,
    }
    wq_full = np.asarray(inputs["L0_attn_W_q"], dtype=np.float32)
    wo_full = np.asarray(inputs["L0_attn_W_o"], dtype=np.float32)
    wuk_full = np.asarray(inputs["L0_attn_W_up_k"], dtype=np.float32)
    wuv_full = np.asarray(inputs["L0_attn_W_up_v"], dtype=np.float32)
    in_maps = []
    for c in range(8):
        b, r = c // 4, c % 4
        m = dict(com)
        m["xin"] = np.ascontiguousarray(x[b, r * TPC:(r + 1) * TPC, :])
        m["wq"] = np.ascontiguousarray(wq_full[:, r * 256:(r + 1) * 256])
        m["wo"] = np.ascontiguousarray(wo_full[r * 256:(r + 1) * 256, :])
        m["wuk"] = np.ascontiguousarray(wuk_full[:, r * 256:(r + 1) * 256])
        m["wuv"] = np.ascontiguousarray(wuv_full[:, r * 256:(r + 1) * 256])
        in_maps.append(m)
    return in_maps


def kernel(**inputs):
    in_maps = make_in_maps(inputs)
    nc, _ = build_nc()
    res = run_bass_kernel_spmd(nc, in_maps, core_ids=list(range(8)))
    out = np.empty((B, S, D), dtype=np.float32)
    for c in range(8):
        b, r = c // 4, c % 4
        out[b, r * TPC:(r + 1) * TPC, :] = res.results[c]["y"]
    return out
